# version 01
# baseline (speedup 1.0000x reference)
import numpy as np
import concourse.bass as bass
import concourse.mybir as mybir
from concourse.bass_utils import run_bass_kernel_spmd

F32 = mybir.dt.float32
BF16 = mybir.dt.bfloat16
U8 = mybir.dt.uint8
A = mybir.AluOpType
AF = mybir.ActivationFunctionType
AX = mybir.AxisListType

D = 2048
NCH = D // 128
NH = 16
B = 2
L = 8192
NCORES = 8
HALO = 256
TCORE = 2048
TTOT = HALO + TCORE
TT = 256
NTILES = TTOT // TT
C = 128
SCALE = 128.0 ** -0.5
EPS_LN = 1e-5
EPS_RMS_FFN = 1e-8
EPS_FFN = 1e-8
EPS_ATTN = 1e-5
QB = 127.0
LADDER_BITS = [128.0, 64.0, 32.0, 16.0, 8.0, 4.0, 2.0, 1.0]
WT_BITS = [4.0, 2.0, 1.0]

_DBG = []


class Plan:
    def __init__(self):
        self.ops = []
        self.writer = {}
        self.readers = {}

    def op(self, engine, fn, reads=(), writes=(), dma=False, narrow=False):
        deps = set()
        for bb in reads:
            w = self.writer.get(bb)
            if w is not None:
                deps.add(w)
        for bb in writes:
            w = self.writer.get(bb)
            if w is not None:
                deps.add(w)
            for r in self.readers.get(bb, ()):
                deps.add(r)
        i = len(self.ops)
        self.ops.append(dict(engine=engine, fn=fn, deps=deps, dma=dma,
                             narrow=narrow))
        for bb in reads:
            self.readers.setdefault(bb, []).append(i)
        for bb in writes:
            self.writer[bb] = i
            self.readers[bb] = []
        return i

    def emit(self, nc, sems):
        ops = self.ops
        engseq = [0] * len(ops)
        cnt = {}
        for i, o in enumerate(ops):
            engseq[i] = cnt.get(o["engine"], 0)
            cnt[o["engine"]] = engseq[i] + 1
        interlock = [set() for _ in ops]
        needs_inc = [False] * len(ops)
        for i, o in enumerate(ops):
            for d in o["deps"]:
                if ops[d]["dma"] or ops[d]["engine"] != o["engine"]:
                    needs_inc[d] = True
                elif ((ops[d]["narrow"] or o["narrow"])
                      and engseq[i] - engseq[d] <= 3):
                    needs_inc[d] = True
                    interlock[i].add(d)
        for i, o in enumerate(ops):
            if o["dma"]:
                needs_inc[i] = True
        last_of = {}
        for i, o in enumerate(ops):
            last_of[o["engine"]] = i
        for i in last_of.values():
            needs_inc[i] = True
        cum = {}
        opcum = [0] * len(ops)
        for i, o in enumerate(ops):
            eng = o["engine"]
            amt = 16 if o["dma"] else 1
            if needs_inc[i]:
                cum[eng] = cum.get(eng, 0) + amt
            opcum[i] = cum.get(eng, 0)
        total = dict(cum)
        hiwater = {}

        def emit_engine(eng_name, eng):
            hw = hiwater.setdefault(eng_name, {})
            for i, o in enumerate(ops):
                if o["engine"] != eng_name:
                    continue
                for d in sorted(o["deps"]):
                    do = ops[d]
                    if (do["engine"] == eng_name and not do["dma"]
                            and d not in interlock[i]):
                        continue
                    if not needs_inc[d]:
                        raise RuntimeError("dep without inc")
                    sem = sems[do["engine"]]
                    val = opcum[d]
                    if hw.get(do["engine"], 0) < val:
                        eng.wait_ge(sem, val)
                        hw[do["engine"]] = val
                ins = o["fn"](eng)
                if needs_inc[i]:
                    ins.then_inc(sems[eng_name], 16 if o["dma"] else 1)
        return emit_engine, total


def _ladder_floor(p, v_, u0, uu, u2, bits, keys, pre_add, eng="vector"):
    kv, k0, ku, k2 = keys
    p.op(eng, lambda e, a=v_, o=u0, c=pre_add: e.tensor_scalar(o, a, c, None, A.add),
         reads=[kv], writes=[k0])
    first = True
    for bit in bits:
        src = u0 if first else uu
        ksrc = k0 if first else ku
        p.op(eng, lambda e, s=src, o=u2, bb=bit: e.tensor_scalar(o, s, bb, -bb, A.is_ge, A.mult),
             reads=[ksrc], writes=[k2])
        p.op(eng, lambda e, s=src, o=uu, c=u2: e.tensor_tensor(o, s, c, A.add),
             reads=[ksrc, k2], writes=[ku])
        first = False
    p.op(eng, lambda e, o=u2, x=u0, y=uu: e.tensor_tensor(o, x, y, A.subtract),
         reads=[k0, ku], writes=[k2])


def build_program():
    nc = bass.Bass("TRN2", target_bir_lowering=False)

    x_fm = nc.dram_tensor("x_fm", [D, TTOT], F32, kind="ExternalInput").ap()
    wq = nc.dram_tensor("wq", [D, D], F32, kind="ExternalInput").ap()
    wk = nc.dram_tensor("wk", [D, D], F32, kind="ExternalInput").ap()
    wi = nc.dram_tensor("wi", [D, D], F32, kind="ExternalInput").ap()
    wo = nc.dram_tensor("wo", [D, D], F32, kind="ExternalInput").ap()
    fw = nc.dram_tensor("fw", [D, D], F32, kind="ExternalInput").ap()
    fbias = nc.dram_tensor("fbias", [128, NCH], F32, kind="ExternalInput").ap()
    gwin = nc.dram_tensor("gwin", [128, NCH], F32, kind="ExternalInput").ap()
    ident_in = nc.dram_tensor("ident_in", [128, 128], F32, kind="ExternalInput").ap()
    causal_in = nc.dram_tensor("causal_in", [128, 128], U8, kind="ExternalInput").ap()
    halom_in = nc.dram_tensor("halom_in", [128, 2], F32, kind="ExternalInput").ap()
    onesc_in = nc.dram_tensor("onesc_in", [128, 128], F32, kind="ExternalInput").ap()
    y_fm = nc.dram_tensor("y_fm", [D, TCORE], F32, kind="ExternalOutput").ap()
    wscr = [nc.dram_tensor(f"wscr{i}", [2 * D, D], BF16, kind="Internal").ap()
            for i in range(4)]
    dbg_t = {}
    for name in _DBG:
        dbg_t[name] = nc.dram_tensor("dbg_" + name, [128, 4096], F32,
                                     kind="ExternalOutput").ap()

    from contextlib import ExitStack
    es = ExitStack()

    def sb(name, shape, dt=F32):
        return es.enter_context(nc.sbuf_tensor(name, shape, dt))

    xres = sb("xres", [128, NCH * TT])
    xno = sb("xno", [128, NCH * TT])
    vbuf = sb("vbuf", [128, NCH * TT])
    kraw = sb("kraw", [128, NCH * TT])
    hixq = sb("hixq", [128, NCH * TT], BF16)
    lobf = sb("lobf", [128, NCH * TT], BF16)
    wtT = sb("wtT", [128, NCH * D], BF16)
    wsl = [sb(f"wsl{i}", [128, D], BF16) for i in range(2)]
    qhalf = sb("qhalf", [128, 8 * TT])
    kh = sb("kh", [128, TT])
    gh = sb("gh", [128, TT])
    th = sb("th", [128, TT])
    bt = sb("bt", [128, C])
    e1 = sb("e1", [128, C])
    e2 = sb("e2", [128, C])
    qint = sb("qint", [128, C])
    qitr = sb("qitr", [128, C])
    kot = sb("kot", [128, C])
    Pp = sb("Pp", [128, C])
    kTs = sb("kTs", [128, C])
    vTs = sb("vTs", [128, C])
    Sst = sb("Sst", [128, NH * 128])
    st8 = sb("st8", [128, 8])
    stE = sb("stE", [128, 4])
    sq = sb("sq", [128, 2 * TT])
    u0 = sb("u0", [128, TT])
    uu = sb("uu", [128, TT])
    u2 = sb("u2", [128, TT])
    rows = sb("rows", [1, 8 * TT])
    rowx = sb("rowx", [1, 2 * TT])
    colsb = sb("colsb", [128, 16])
    rb = sb("rb", [128, TT])
    ab = sb("ab", [128, TT])
    xsb = sb("xsb", [128, TT])
    invsb = sb("invsb", [128, TT])
    gamma = sb("gamma", [128, NCH])
    hbias = sb("hbias", [128, NCH])
    gwb = sb("gwb", [128, NCH])
    ident = sb("ident", [128, 128])
    causal = sb("causal", [128, 128], U8)
    halom = sb("halom", [128, 2])
    onesc = sb("onesc", [128, 128])
    Mmax = sb("Mmax", [128, TT])
    Mmin = sb("Mmin", [128, TT])
    ev = sb("ev", [128, 2 * TT])
    ev2 = sb("ev2", [128, 2 * TT])

    PS = es.enter_context(nc.psum_tensor("PS", [128, 4096], F32))
    ps_pj = [PS[:, n * TT:(n + 1) * TT] for n in range(16)]
    ps_bk = [PS[:, j * 512:j * 512 + TT] for j in range(8)]
    ps_A = PS[:, 0:128]
    ps_O = PS[:, 512:640]
    ps_T = PS[:, 1024:1152]
    ps_S = PS[:, 1536:1664]
    ps_bc1 = PS[:, 4 * 512:4 * 512 + TT]
    ps_bc2 = PS[:, 5 * 512:5 * 512 + TT]
    ps_tr = PS[:, 6 * 512:6 * 512 + 128]
    ps_rr = PS[0:1, 6 * 512 + 128:6 * 512 + 256]
    ps_row2 = PS[0:1, 6 * 512 + 256:6 * 512 + 512]
    ps_row1 = PS[0:1, 7 * 512 + 256:7 * 512 + 512]

    sems = {}
    sem_names = ["sync", "tensor", "vector", "scalar", "gpsimd"]
    for s in sem_names:
        sems[s] = es.enter_context(nc.semaphore("sem_" + s))

    p = Plan()

    def ch(buf, c, w=TT):
        return buf[:, c * w:(c + 1) * w]

    def row(i):
        return rows[0:1, i * TT:(i + 1) * TT]

    for dst, src, key in [(ident, ident_in, "ident"), (causal, causal_in, "causal"),
                          (halom, halom_in, "halom"), (onesc, onesc_in, "onesc"),
                          (gwb, gwin, "gwb"), (hbias, fbias, "hbias_raw")]:
        p.op("sync", lambda e, d=dst, s=src: e.dma_start(out=d[:, :], in_=s),
             writes=[key], dma=True)
    p.op("vector", lambda e: e.tensor_scalar(hbias[:, :], hbias[:, :], 0.5, None, A.mult),
         reads=["hbias_raw"], writes=["hbias"], narrow=True)
    p.op("vector", lambda e: e.memset(Pp[:, :], 0.0), writes=["Pp_z"])
    p.op("vector", lambda e: e.memset(Sst[:, :], 0.0), writes=["S"])

    ones_col = onesc[:, 0:1]
    ones_row = onesc[0:1, :]

    for widx, wsrc in enumerate([wq, wk, wi, wo]):
        for k in range(NCH):
            p.op("sync", lambda e, s=wsrc, k=k: e.dma_start(
                out=vbuf[:, 0:D], in_=s[k * 128:(k + 1) * 128, :]),
                reads=[], writes=["prep_w"], dma=True)
            p.op("vector", lambda e: e.tensor_copy(hixq[:, 0:D], vbuf[:, 0:D]),
                 reads=["prep_w"], writes=["prep_hi"])
            p.op("vector", lambda e: e.tensor_tensor(lobf[:, 0:D], vbuf[:, 0:D],
                                                     hixq[:, 0:D], A.subtract),
                 reads=["prep_w", "prep_hi"], writes=["prep_lo"])
            for nh in range(2):
                r0 = nh * D + k * 128
                p.op("sync", lambda e, r0=r0, w=widx, nh=nh: e.dma_start(
                    out=wscr[w][r0:r0 + 128, 0:1024],
                    in_=hixq[:, nh * 1024:(nh + 1) * 1024]),
                    reads=["prep_hi"], writes=[], dma=True)
                p.op("sync", lambda e, r0=r0, w=widx, nh=nh: e.dma_start(
                    out=wscr[w][r0:r0 + 128, 1024:2048],
                    in_=lobf[:, nh * 1024:(nh + 1) * 1024]),
                    reads=["prep_lo"], writes=[], dma=True)

    for n in range(NCH):
        p.op("sync", lambda e, n=n: e.dma_start(
            out=vbuf[:, 0:D], in_=fw[n * 128:(n + 1) * 128, :]),
            reads=["prep_hi", "prep_lo"], writes=["prep_w"], dma=True)
        p.op("vector", lambda e: e.reduce_sum(colsb[:, 1:2], vbuf[:, 0:D], AX.X,
                                              apply_absolute_value=True),
             reads=["prep_w"], writes=["c1"], narrow=True)
        p.op("vector", lambda e, n=n: e.tensor_scalar(gamma[:, n:n + 1], colsb[:, 1:2],
                                                      1.0 / 2048.0, None, A.mult),
             reads=["c1"], writes=["gamma"], narrow=True)
        p.op("vector", lambda e, n=n: e.tensor_scalar(colsb[:, 2:3], gamma[:, n:n + 1],
                                                      EPS_FFN, None, A.add),
             reads=["gamma"], writes=["c2"], narrow=True)
        p.op("vector", lambda e: e.reciprocal(colsb[:, 0:1], colsb[:, 2:3]),
             reads=["c2"], writes=["c0"], narrow=True)
        p.op("vector", lambda e: e.tensor_scalar(xno[:, 0:D], vbuf[:, 0:D],
                                                 colsb[:, 0:1], None, A.mult),
             reads=["prep_w", "c0"], writes=["prep_z"])
        for piece in range(D // TT):
            sl = slice(piece * TT, (piece + 1) * TT)
            _ladder_floor(p, xno[:, sl], u0[:, :], uu[:, :], u2[:, :],
                          WT_BITS, ("prep_z", "l0", "l1", "l2"), 2.5)
            p.op("vector", lambda e, sl=sl: e.tensor_scalar(kraw[:, sl], u2[:, :],
                                                            -2.0, -1.0, A.add, A.max),
                 reads=["l2"], writes=["prep_wt"])
            p.op("vector", lambda e, sl=sl: e.tensor_scalar(kraw[:, sl], kraw[:, sl],
                                                            1.0, None, A.min),
                 reads=["prep_wt"], writes=["prep_wt"])
        for j in range(NCH):
            p.op("tensor", lambda e, j=j: e.transpose(ps_tr, kraw[:, j * 128:(j + 1) * 128],
                                                      ident[:, :]),
                 reads=["prep_wt", "ident"], writes=["PSUM"])
            p.op("scalar", lambda e, j=j, n=n: e.copy(
                out=wtT[:, j * D + n * 128:j * D + (n + 1) * 128], in_=ps_tr),
                reads=["PSUM"], writes=["wtT"])

    def ln_stats(tag, srcbuf, srckey, with_rms):
        for c in range(NCH):
            slot = c % 2
            p.op("scalar", lambda e, c=c, s=slot: e.activation(
                out=sq[:, s * TT:(s + 1) * TT], in_=ch(srcbuf, c), func=AF.Square),
                reads=[srckey], writes=[f"sq{slot}"])
            p.op("tensor", lambda e, c=c: e.matmul(
                ps_row1, ones_col, ch(srcbuf, c), start=(c == 0), stop=(c == NCH - 1)),
                reads=[srckey, "onesc"], writes=["PSUM"])
            p.op("tensor", lambda e, c=c, s=slot: e.matmul(
                ps_row2, ones_col, sq[:, s * TT:(s + 1) * TT],
                start=(c == 0), stop=(c == NCH - 1)),
                reads=[f"sq{slot}", "onesc"], writes=["PSUM"])
        p.op("vector", lambda e: e.tensor_scalar(row(0), ps_row1, 1.0 / 2048.0, None, A.mult),
             reads=["PSUM"], writes=["r0"], narrow=True)
        p.op("vector", lambda e: e.tensor_scalar(row(1), ps_row2, 1.0 / 2048.0, None, A.mult),
             reads=["PSUM"], writes=["r1"], narrow=True)
        p.op("vector", lambda e: e.tensor_tensor(row(6), row(0), row(0), A.mult),
             reads=["r0"], writes=["r6"], narrow=True)
        p.op("vector", lambda e: e.tensor_tensor(row(2), row(1), row(6), A.subtract),
             reads=["r1", "r6"], writes=["r2"], narrow=True)
        p.op("vector", lambda e: e.tensor_scalar(row(6), row(2), EPS_LN, None, A.add),
             reads=["r2"], writes=["r6"], narrow=True)
        p.op("scalar", lambda e: e.activation(out=row(7), in_=row(6), func=AF.Ln),
             reads=["r6"], writes=["r7"])
        p.op("vector", lambda e: e.tensor_scalar(row(7), row(7), -0.5, None, A.mult),
             reads=["r7"], writes=["r7"], narrow=True)
        p.op("scalar", lambda e: e.activation(out=row(3), in_=row(7), func=AF.Exp),
             reads=["r7"], writes=["r3"])
        if with_rms:
            p.op("vector", lambda e: e.tensor_tensor(row(6), row(3), row(3), A.mult),
                 reads=["r3"], writes=["r6"], narrow=True)
            p.op("vector", lambda e: e.tensor_tensor(row(6), row(2), row(6), A.mult),
                 reads=["r2", "r6"], writes=["r6"], narrow=True)
            p.op("vector", lambda e: e.tensor_scalar(row(6), row(6), EPS_RMS_FFN, None, A.add),
                 reads=["r6"], writes=["r6"], narrow=True)
            p.op("scalar", lambda e: e.activation(out=row(7), in_=row(6), func=AF.Ln),
                 reads=["r6"], writes=["r7"])
            p.op("vector", lambda e: e.tensor_scalar(row(7), row(7), -0.5, None, A.mult),
                 reads=["r7"], writes=["r7"], narrow=True)
            p.op("scalar", lambda e: e.activation(out=row(6), in_=row(7), func=AF.Exp),
                 reads=["r7"], writes=["r6"])
            p.op("vector", lambda e: e.tensor_tensor(row(4), row(3), row(6), A.mult),
                 reads=["r3", "r6"], writes=["r4"], narrow=True)
        else:
            p.op("vector", lambda e: e.tensor_copy(row(4), row(3)),
                 reads=["r3"], writes=["r4"], narrow=True)
        p.op("vector", lambda e: e.tensor_tensor(row(5), row(0), row(4), A.mult),
             reads=["r0", "r4"], writes=["r5"], narrow=True)
        p.op("tensor", lambda e: e.matmul(ps_bc1, ones_row, row(4), start=True, stop=True),
             reads=["r4", "onesc"], writes=["PSUM"])
        p.op("tensor", lambda e: e.matmul(ps_bc2, ones_row, row(5), start=True, stop=True),
             reads=["r5", "onesc"], writes=["PSUM"])
        p.op("scalar", lambda e: e.copy(out=rb[:, :], in_=ps_bc1),
             reads=["PSUM"], writes=["rb"])
        p.op("scalar", lambda e: e.copy(out=ab[:, :], in_=ps_bc2),
             reads=["PSUM"], writes=["ab"])

    def ln_apply(srcbuf, srckey, dstbuf, dstkey):
        for c in range(NCH):
            eng = "gpsimd" if c % 2 == 0 else "vector"
            p.op(eng, lambda e, c=c: e.tensor_tensor(ch(dstbuf, c), ch(srcbuf, c),
                                                     rb[:, :], A.mult),
                 reads=[srckey, "rb"], writes=[dstkey])
            p.op(eng, lambda e, c=c: e.tensor_tensor(ch(dstbuf, c), ch(dstbuf, c),
                                                     ab[:, :], A.subtract),
                 reads=[dstkey, "ab"], writes=[dstkey])

    def absmax_quant(srcbuf, srckey):
        p.op("vector", lambda e: e.tensor_tensor(Mmax[:, :], ch(srcbuf, 0), ch(srcbuf, 1), A.max),
             reads=[srckey], writes=["Mmax"])
        p.op("gpsimd", lambda e: e.tensor_copy(Mmin[:, :], ch(srcbuf, 0)),
             reads=[srckey], writes=["Mmin"])
        for c in range(1, NCH):
            if c > 1:
                p.op("vector", lambda e, c=c: e.tensor_tensor(Mmax[:, :], Mmax[:, :],
                                                              ch(srcbuf, c), A.max),
                     reads=[srckey, "Mmax"], writes=["Mmax"])
            p.op("vector", lambda e, c=c: e.tensor_tensor(Mmin[:, :], Mmin[:, :],
                                                          ch(srcbuf, c), A.min),
                 reads=[srckey, "Mmin"], writes=["Mmin"])
        for j in range(TT // 128):
            p.op("tensor", lambda e, j=j: e.transpose(ps_tr, Mmax[:, j * 128:(j + 1) * 128],
                                                      ident[:, :]),
                 reads=["Mmax", "ident"], writes=["PSUM"])
            p.op("vector", lambda e, j=j: e.reduce_max(colsb[:, j:j + 1], ps_tr, AX.X),
                 reads=["PSUM"], writes=["cmax"], narrow=True)
            p.op("tensor", lambda e, j=j: e.transpose(ps_tr, Mmin[:, j * 128:(j + 1) * 128],
                                                      ident[:, :]),
                 reads=["Mmin", "PSUM", "cmax"], writes=["PSUM"])
            p.op("vector", lambda e, j=j: e.tensor_reduce(colsb[:, 2 + j:3 + j], ps_tr, AX.X, A.min),
                 reads=["PSUM"], writes=["cmin"], narrow=True)
        for j in range(TT // 128):
            p.op("vector", lambda e, j=j: e.tensor_scalar(colsb[:, 4 + j:5 + j],
                                                          colsb[:, 2 + j:3 + j], -1.0, None, A.mult),
                 reads=["cmin"], writes=["cneg"], narrow=True)
            p.op("vector", lambda e, j=j: e.tensor_tensor(colsb[:, 6 + j:7 + j],
                                                          colsb[:, j:j + 1],
                                                          colsb[:, 4 + j:5 + j], A.max),
                 reads=["cmax", "cneg"], writes=["cam"], narrow=True)
            p.op("vector", lambda e, j=j: e.tensor_scalar(colsb[:, 8 + j:9 + j],
                                                          colsb[:, 6 + j:7 + j],
                                                          1.0 / QB, None, A.mult),
                 reads=["cam"], writes=["cxs"], narrow=True)
            p.op("vector", lambda e, j=j: e.tensor_scalar(colsb[:, 10 + j:11 + j],
                                                          colsb[:, 8 + j:9 + j],
                                                          EPS_FFN, None, A.add),
                 reads=["cxs"], writes=["cpe"], narrow=True)
            p.op("vector", lambda e, j=j: e.reciprocal(
                colsb[:, 12 + j:13 + j], colsb[:, 10 + j:11 + j]),
                reads=["cpe"], writes=["cinv"], narrow=True)
        for j in range(TT // 128):
            p.op("tensor", lambda e, j=j: e.transpose(ps_rr, colsb[:, 8 + j:9 + j], ident[:, :]),
                 reads=["cxs", "ident"], writes=["PSUM"])
            p.op("scalar", lambda e, j=j: e.copy(out=rowx[0:1, j * 128:(j + 1) * 128], in_=ps_rr),
                 reads=["PSUM"], writes=["rowxs"])
            p.op("tensor", lambda e, j=j: e.transpose(ps_rr, colsb[:, 12 + j:13 + j], ident[:, :]),
                 reads=["cinv", "ident", "rowxs"], writes=["PSUM"])
            p.op("scalar", lambda e, j=j: e.copy(out=rowx[0:1, TT + j * 128:TT + (j + 1) * 128],
                                                 in_=ps_rr),
                 reads=["PSUM"], writes=["rowinv"])
        p.op("tensor", lambda e: e.matmul(ps_bc1, ones_row, rowx[0:1, 0:TT], start=True, stop=True),
             reads=["rowxs", "onesc"], writes=["PSUM"])
        p.op("scalar", lambda e: e.copy(out=xsb[:, :], in_=ps_bc1),
             reads=["PSUM"], writes=["xsb"])
        p.op("tensor", lambda e: e.matmul(ps_bc2, ones_row, rowx[0:1, TT:2 * TT], start=True, stop=True),
             reads=["rowinv", "onesc"], writes=["PSUM"])
        p.op("scalar", lambda e: e.copy(out=invsb[:, :], in_=ps_bc2),
             reads=["PSUM"], writes=["invsb"])
        for c in range(NCH):
            p.op("vector", lambda e, c=c: e.tensor_tensor(uu[:, :], ch(srcbuf, c),
                                                          invsb[:, :], A.mult),
                 reads=[srckey, "invsb"], writes=["lq"])
            _ladder_floor(p, uu[:, :], u0[:, :], uu[:, :], u2[:, :],
                          LADDER_BITS, ("lq", "l0", "lq", "l2"), 128.5)
            p.op("vector", lambda e, c=c: e.tensor_scalar(u2[:, :], u2[:, :],
                                                          -128.0, -127.0, A.add, A.max),
                 reads=["l2"], writes=["l2"])
            p.op("vector", lambda e, c=c: e.tensor_scalar(ch(hixq, c), u2[:, :],
                                                          127.0, None, A.min),
                 reads=["l2"], writes=["hixq"])

    def ffn_matmul_evac(tag):
        for n in range(NCH):
            for k in range(NCH):
                p.op("tensor", lambda e, k=k, n=n: e.matmul(
                    ps_pj[n], wtT[:, k * D + n * 128:k * D + (n + 1) * 128],
                    ch(hixq, k), start=(k == 0), stop=(k == NCH - 1)),
                    reads=["hixq", "wtT"], writes=["PSUM"])
        for n in range(NCH):
            slot = n % 2
            p.op("scalar", lambda e, n=n, s=slot: e.mul(
                ev[:, s * TT:(s + 1) * TT], ps_pj[n], gamma[:, n:n + 1]),
                reads=["PSUM", "gamma"], writes=[f"ev{slot}"])
            p.op("vector", lambda e, n=n, s=slot: e.tensor_tensor(
                ev2[:, s * TT:(s + 1) * TT], ev[:, s * TT:(s + 1) * TT], xsb[:, :], A.mult),
                reads=[f"ev{slot}", "xsb"], writes=[f"e2{slot}"])
            p.op("vector", lambda e, n=n, s=slot: e.tensor_scalar(
                ev2[:, s * TT:(s + 1) * TT], ev2[:, s * TT:(s + 1) * TT],
                0.5, hbias[:, n:n + 1], A.mult, A.add),
                reads=[f"e2{slot}", "hbias"], writes=[f"e2{slot}"])
            p.op("vector", lambda e, n=n, s=slot: e.tensor_tensor(
                ch(xres, n), ch(xres, n), ev2[:, s * TT:(s + 1) * TT], A.add),
                reads=[f"e2{slot}", "xres"], writes=["xres"])

    def stream_proj(widx, nhalf, evac, tag=""):
        for k in range(NCH):
            slot = k % 2
            wkey = f"wsl{slot}"
            r0 = nhalf * D + k * 128
            p.op("sync", lambda e, r0=r0, s=slot, w=widx: e.dma_start(
                out=wsl[s][:, :], in_=wscr[w][r0:r0 + 128, :]),
                reads=[wkey + "_r"], writes=[wkey], dma=True)
            for j in range(8):
                whi = wsl[slot][:, j * 128:(j + 1) * 128]
                wlo = wsl[slot][:, 1024 + j * 128:1024 + (j + 1) * 128]
                xh = ch(hixq, k)
                xl = ch(lobf, k)
                p.op("tensor", lambda e, o=ps_bk[j], a=whi, bb=xh, f=(k == 0): e.matmul(
                    o, a, bb, start=f, stop=False),
                    reads=[wkey, "hixq"], writes=["PSUM"])
                p.op("tensor", lambda e, o=ps_bk[j], a=whi, bb=xl: e.matmul(
                    o, a, bb, start=False, stop=False),
                    reads=[wkey, "lobf"], writes=["PSUM"])
                p.op("tensor", lambda e, o=ps_bk[j], a=wlo, bb=xh,
                     last=(k == NCH - 1): e.matmul(o, a, bb, start=False, stop=last),
                     reads=[wkey, "hixq"], writes=["PSUM"])
            p.op("tensor", lambda e: e.nop(), reads=["PSUM"], writes=[wkey + "_r"])
        for j in range(8):
            evac(nhalf * 8 + j, ps_bk[j])

    def gla_head(h, chunks, has_q, tile_idx):
        p.op("scalar", lambda e, h=h: e.activation(out=th[:, :], in_=ch(kraw, h), func=AF.Exp),
             reads=["kraw"], writes=["th"])
        p.op("vector", lambda e: e.tensor_scalar(th[:, :], th[:, :], 1.0, None, A.add),
             reads=["th"], writes=["th"])
        p.op("scalar", lambda e: e.activation(out=gh[:, :], in_=th[:, :], func=AF.Ln),
             reads=["th"], writes=["gh"])
        p.op("vector", lambda e, h=h: e.tensor_tensor(th[:, :], ch(kraw, h), gh[:, :], A.subtract),
             reads=["kraw", "gh"], writes=["th"])
        p.op("scalar", lambda e: e.activation(out=kh[:, :], in_=th[:, :], func=AF.Exp),
             reads=["th"], writes=["kh"])
        p.op("vector", lambda e: e.tensor_scalar(gh[:, :], gh[:, :], -1.0, None, A.mult),
             reads=["gh"], writes=["gh"])
        def _tap_kg():
            if "gla0" in dbg_t and h == 0 and tile_idx == 1:
                p.op("sync", lambda e, d=dbg_t["gla0"]: e.dma_start(
                    out=d[:, 9 * 128:11 * 128], in_=kh[:, 0:256]),
                    reads=["kh"], writes=[], dma=True)
                p.op("sync", lambda e, d=dbg_t["gla0"]: e.dma_start(
                    out=d[:, 11 * 128:13 * 128], in_=gh[:, 0:256]),
                    reads=["gh"], writes=[], dma=True)
        _tap_kg()
        Sh = Sst[:, h * 128:(h + 1) * 128]
        def gtap(i, buf, key):
            if "gla0" in dbg_t and h == 0 and tile_idx == 1:
                p.op("sync", lambda e, d=dbg_t["gla0"], s=buf, i=i: e.dma_start(
                    out=d[:, i * 128:(i + 1) * 128], in_=s[:, 0:128]),
                    reads=[key], writes=[], dma=True)
        for c in chunks:
            csl = slice(c * C, (c + 1) * C)
            p.op("tensor", lambda e, h=h, csl=csl: e.transpose(
                ps_T, vbuf[:, h * TT:(h + 1) * TT][:, csl.start % TT:csl.start % TT + C],
                ident[:, :]),
                reads=["vbuf", "ident"], writes=["PSUM"])
            p.op("scalar", lambda e: e.copy(out=vTs[:, :], in_=ps_T),
                 reads=["PSUM"], writes=["vTs"])
            if c == 0: gtap(7, vTs, "vTs")
            if tile_idx == 0:
                p.op("vector", lambda e, c=c: e.tensor_scalar(
                    vTs[:, :], vTs[:, :], halom[:, c:c + 1], None, A.mult),
                    reads=["vTs", "halom"], writes=["vTs"])
            p.op("vector", lambda e, csl=csl: e.tensor_tensor_scan(
                bt[:, :], gh[:, csl], gh[:, csl], 0.0, A.add, A.bypass),
                reads=["gh"], writes=["bt"])
            if c == 0: gtap(0, bt, "bt")
            p.op("vector", lambda e: e.tensor_copy(st8[:, 0:1], bt[:, C - 1:C]),
                 reads=["bt"], writes=["st8"], narrow=True)
            p.op("vector", lambda e: e.tensor_scalar(st8[:, 1:2], st8[:, 0:1], 0.5, None, A.mult),
                 reads=["st8"], writes=["st8"], narrow=True)
            p.op("vector", lambda e: e.tensor_scalar(st8[:, 2:3], st8[:, 0:1], -0.5, None, A.mult),
                 reads=["st8"], writes=["st8"], narrow=True)
            p.op("scalar", lambda e: e.activation(out=stE[:, 0:3], in_=st8[:, 0:3], func=AF.Exp),
                 reads=["st8"], writes=["stE"])
            p.op("scalar", lambda e: e.activation(out=e1[:, :], in_=bt[:, :], func=AF.Exp),
                 reads=["bt"], writes=["e1"])
            if c == 0: gtap(1, e1, "e1")
            if has_q:
                qsl = qhalf[:, (h % 8) * TT:((h % 8) + 1) * TT][:, csl.start % TT:csl.start % TT + C]
                p.op("vector", lambda e, q=qsl: e.tensor_tensor(qint[:, :], q, e1[:, :], A.mult),
                     reads=["qhalf", "e1"], writes=["qint"])
                if c == 0: gtap(3, qint, "qint")
            p.op("vector", lambda e: e.tensor_scalar(bt[:, :], bt[:, :], -1.0,
                                                     st8[:, 1:2], A.mult, A.add),
                 reads=["bt", "st8"], writes=["bt"], narrow=True)
            p.op("vector", lambda e: e.tensor_scalar(bt[:, :], bt[:, :], 80.0, None, A.min),
                 reads=["bt"], writes=["bt"])
            p.op("scalar", lambda e: e.activation(out=e2[:, :], in_=bt[:, :], func=AF.Exp),
                 reads=["bt"], writes=["e2"])
            if c == 0: gtap(2, e2, "e2")
            if has_q:
                p.op("scalar", lambda e: e.activation(out=e1[:, :], in_=bt[:, :],
                                                      func=AF.Exp, scale=-1.0),
                     reads=["bt"], writes=["e1"])
                p.op("vector", lambda e, q=qsl: e.tensor_tensor(qitr[:, :], q, e1[:, :], A.mult),
                     reads=["qhalf", "e1"], writes=["qitr"])
                if c == 0: gtap(4, qitr, "qitr")
            p.op("vector", lambda e, csl=csl: e.tensor_tensor(kot[:, :], kh[:, csl],
                                                              e2[:, :], A.mult),
                 reads=["kh", "e2"], writes=["kot"])
            if c == 0: gtap(5, kot, "kot")
            if has_q:
                p.op("tensor", lambda e: e.matmul(ps_A, kot[:, :], qitr[:, :],
                                                  start=True, stop=True),
                     reads=["kot", "qitr"], writes=["PSUM"])
                p.op("vector", lambda e: e.copy_predicated(Pp[:, :], causal[:, :], ps_A),
                     reads=["PSUM", "causal", "Pp_z"], writes=["Pp"])
                if c == 0: gtap(6, Pp, "Pp")
                p.op("tensor", lambda e: e.matmul(ps_O, vTs[:, :], Pp[:, :],
                                                  start=True, stop=False),
                     reads=["vTs", "Pp"], writes=["PSUM"])
                p.op("tensor", lambda e, S=Sh: e.matmul(ps_O, S, qint[:, :],
                                                        start=False, stop=True),
                     reads=["S", "qint"], writes=["PSUM"])
                p.op("scalar", lambda e, h=h, c=c: e.copy(
                    out=xno[:, h * TT + (c % 2) * C:h * TT + (c % 2) * C + C], in_=ps_O),
                    reads=["PSUM"], writes=["xno"])
            p.op("vector", lambda e: e.tensor_scalar(kot[:, :], kot[:, :],
                                                     stE[:, 1:2], None, A.mult),
                 reads=["kot", "stE"], writes=["kot"])
            p.op("tensor", lambda e: e.transpose(ps_T, kot[:, :], ident[:, :]),
                 reads=["kot", "ident", "vTs"], writes=["PSUM"])
            p.op("scalar", lambda e: e.copy(out=kTs[:, :], in_=ps_T),
                 reads=["PSUM"], writes=["kTs"])
            if c == 0: gtap(8, kTs, "kTs")
            p.op("tensor", lambda e: e.matmul(ps_S, kTs[:, :], vTs[:, :],
                                              start=True, stop=True),
                 reads=["kTs", "vTs"], writes=["PSUM"])
            p.op("vector", lambda e, S=Sh: e.tensor_scalar(S, S, stE[:, 0:1], None, A.mult),
                 reads=["S", "stE"], writes=["S"])
            p.op("vector", lambda e, S=Sh: e.tensor_tensor(S, S, ps_S, A.add),
                 reads=["S", "PSUM"], writes=["S"])

    def dbg_dump(name, buf, key):
        if name in dbg_t:
            w = buf[:, :].shape[1]
            p.op("sync", lambda e, d=dbg_t[name], s=buf, w=w: e.dma_start(
                out=d[:, 0:w], in_=s[:, :]),
                reads=[key], writes=[], dma=True)

    for tile in range(NTILES):
        halo_tile = (tile == 0)
        src = x_fm.rearrange("(c p) t -> p c t", p=128)[:, :, tile * TT:(tile + 1) * TT]
        dst = xres[:, :].rearrange("p (c t) -> p c t", t=TT)
        p.op("sync", lambda e, s=src, d=dst: e.dma_start(out=d, in_=s),
             reads=["xres"], writes=["xres"], dma=True)

        if tile == 1:
            dbg_dump("x0", xres, "xres")
        ln_stats("ln1", xres, "xres", with_rms=True)
        ln_apply(xres, "xres", xno, "xno")
        if tile == 1:
            dbg_dump("xn1", xno, "xno")
        absmax_quant(xno, "xno")
        if tile == 1 and "xq1" in dbg_t:
            for c in range(NCH):
                p.op("vector", lambda e, c=c: e.tensor_copy(ch(vbuf, c), ch(hixq, c)),
                     reads=["hixq"], writes=["vbuf"])
            dbg_dump("xq1", vbuf, "vbuf")
        if tile == 1 and "misc1" in dbg_t:
            p.op("vector", lambda e: e.tensor_copy(kraw[:, 0:TT], xsb[:, :]),
                 reads=["xsb"], writes=["kraw"])
            p.op("vector", lambda e: e.tensor_copy(kraw[:, TT:2 * TT], invsb[:, :]),
                 reads=["invsb"], writes=["kraw"])
            p.op("vector", lambda e: e.tensor_copy(kraw[:, 2 * TT:3 * TT], Mmax[:, :]),
                 reads=["Mmax"], writes=["kraw"])
            p.op("vector", lambda e: e.tensor_copy(kraw[:, 3 * TT:4 * TT], Mmin[:, :]),
                 reads=["Mmin"], writes=["kraw"])
            p.op("vector", lambda e: e.tensor_copy(kraw[:, 4 * TT:4 * TT + 16], colsb[:, :]),
                 reads=["cinv", "cxs", "cam"], writes=["kraw"])
            p.op("vector", lambda e: e.tensor_copy(kraw[:, 4 * TT + 16:4 * TT + 32], gamma[:, :]),
                 reads=["gamma"], writes=["kraw"])
            p.op("vector", lambda e: e.tensor_copy(kraw[0:1, 5 * TT:7 * TT], rowx[0:1, :]),
                 reads=["rowxs", "rowinv"], writes=["kraw"])
            dbg_dump("misc1", kraw, "kraw")
        ffn_matmul_evac("ffn1")
        if tile == 1:
            dbg_dump("x1", xres, "xres")

        ln_stats("ln2", xres, "xres", with_rms=False)
        ln_apply(xres, "xres", xno, "xno")
        for c in range(NCH):
            p.op("vector", lambda e, c=c: e.tensor_copy(ch(hixq, c), ch(xno, c)),
                 reads=["xno"], writes=["hixq"])
            p.op("vector", lambda e, c=c: e.tensor_tensor(ch(lobf, c), ch(xno, c),
                                                          ch(hixq, c), A.subtract),
                 reads=["xno", "hixq"], writes=["lobf"])

        def evac_v(n, ps):
            p.op("scalar", lambda e, n=n, ps=ps: e.copy(out=ch(vbuf, n), in_=ps),
                 reads=["PSUM"], writes=["vbuf"])
        for nh in range(2):
            stream_proj(2, nh, evac_v, tag="v")
        if tile == 1:
            dbg_dump("v1", vbuf, "vbuf")

        def evac_k(n, ps):
            p.op("scalar", lambda e, n=n, ps=ps: e.copy(out=ch(kraw, n), in_=ps),
                 reads=["PSUM"], writes=["kraw"])
        for nh in range(2):
            stream_proj(1, nh, evac_k, tag="k")
        if tile == 1:
            dbg_dump("kraw1", kraw, "kraw")

        if halo_tile:
            for h in range(NH):
                gla_head(h, [0, 1], has_q=False, tile_idx=tile)
            dbg_dump("S0", Sst, "S")
            dbg_dump("v0", vbuf, "vbuf")
            dbg_dump("kraw0", kraw, "kraw")
        else:
            for half in range(2):
                def evac_q(n, ps):
                    p.op("scalar", lambda e, n=n, ps=ps: e.activation(
                        out=qhalf[:, (n % 8) * TT:((n % 8) + 1) * TT], in_=ps,
                        func=AF.Relu, scale=SCALE),
                        reads=["PSUM"], writes=["qhalf"])
                stream_proj(0, half, evac_q, tag="q")
                for h in range(8 * half, 8 * half + 8):
                    gla_head(h, [0, 1], has_q=True, tile_idx=tile)
            if tile == 1:
                dbg_dump("o1", xno, "xno")

            for c in range(NCH):
                slot = c % 2
                p.op("scalar", lambda e, c=c, s=slot: e.activation(
                    out=sq[:, s * TT:(s + 1) * TT], in_=ch(xno, c), func=AF.Square),
                    reads=["xno"], writes=[f"sq{slot}"])
                p.op("tensor", lambda e, c=c, s=slot: e.matmul(
                    ps_row2, ones_col, sq[:, s * TT:(s + 1) * TT],
                    start=(c == 0), stop=(c == NCH - 1)),
                    reads=[f"sq{slot}", "onesc"], writes=["PSUM"])
            p.op("vector", lambda e: e.tensor_scalar(row(6), ps_row2, 1.0 / 2048.0, None, A.mult),
                 reads=["PSUM"], writes=["r6"])
            p.op("vector", lambda e: e.tensor_scalar(row(6), row(6), EPS_ATTN, None, A.add),
                 reads=["r6"], writes=["r6"], narrow=True)
            p.op("scalar", lambda e: e.activation(out=row(7), in_=row(6), func=AF.Ln),
                 reads=["r6"], writes=["r7"])
            p.op("vector", lambda e: e.tensor_scalar(row(7), row(7), -0.5, None, A.mult),
                 reads=["r7"], writes=["r7"], narrow=True)
            p.op("scalar", lambda e: e.activation(out=row(6), in_=row(7), func=AF.Exp),
                 reads=["r7"], writes=["r6"])
            p.op("tensor", lambda e: e.matmul(ps_bc1, ones_row, row(6), start=True, stop=True),
                 reads=["r6", "onesc"], writes=["PSUM"])
            p.op("scalar", lambda e: e.copy(out=rb[:, :], in_=ps_bc1),
                 reads=["PSUM"], writes=["rb"])
            for c in range(NCH):
                slot = c % 2
                p.op("vector", lambda e, c=c, s=slot: e.tensor_tensor(
                    ev[:, s * TT:(s + 1) * TT], ch(xno, c), rb[:, :], A.mult),
                    reads=["xno", "rb"], writes=[f"ev{slot}"])
                p.op("vector", lambda e, c=c, s=slot: e.tensor_scalar(
                    ev2[:, s * TT:(s + 1) * TT], ev[:, s * TT:(s + 1) * TT],
                    gwb[:, c:c + 1], None, A.mult),
                    reads=[f"ev{slot}", "gwb"], writes=[f"e2{slot}"])
                p.op("vector", lambda e, c=c, s=slot: e.tensor_copy(
                    ch(hixq, c), ev2[:, s * TT:(s + 1) * TT]),
                    reads=[f"e2{slot}"], writes=["hixq"])
                p.op("vector", lambda e, c=c, s=slot: e.tensor_tensor(
                    ch(lobf, c), ev2[:, s * TT:(s + 1) * TT], ch(hixq, c), A.subtract),
                    reads=[f"e2{slot}", "hixq"], writes=["lobf"])

            def evac_wo(n, ps):
                p.op("vector", lambda e, n=n, ps=ps: e.tensor_tensor(
                    ch(xres, n), ch(xres, n), ps, A.add),
                    reads=["PSUM", "xres"], writes=["xres"])
            for nh in range(2):
                stream_proj(3, nh, evac_wo, tag="wo")
            if tile == 1:
                dbg_dump("x2", xres, "xres")

            ln_stats("ln3", xres, "xres", with_rms=True)
            ln_apply(xres, "xres", xno, "xno")
            absmax_quant(xno, "xno")
            ffn_matmul_evac("ffn2")

            osrc = xres[:, :].rearrange("p (c t) -> p c t", t=TT)
            odst = y_fm.rearrange("(c p) t -> p c t", p=128)[
                :, :, (tile - 1) * TT:tile * TT]
            p.op("sync", lambda e, s=osrc, d=odst: e.dma_start(out=d, in_=s),
                 reads=["xres", "outdma"], writes=["outdma"], dma=True)

    p.op("sync", lambda e: e.nop(),
         reads=["outdma", "xres", "xno", "PSUM", "S", "wtT"], writes=[])

    emit_engine, totals = p.emit(nc, sems)
    with nc.Block() as blk:
        blk.sync(lambda e: emit_engine("sync", e))
        blk.tensor(lambda e: emit_engine("tensor", e))
        blk.vector(lambda e: emit_engine("vector", e))
        blk.scalar(lambda e: emit_engine("scalar", e))
        blk.gpsimd(lambda e: emit_engine("gpsimd", e))
    es.close()
    return nc


def kernel(L0_attn_hidden_states, L0_attn_Wq, L0_attn_Wk, L0_attn_Wi, L0_attn_Wo,
           L0_attn_g_weight, L0_ffn_weight, L0_ffn_bias):
    x = np.asarray(L0_attn_hidden_states, dtype=np.float32)
    wq = np.ascontiguousarray(np.asarray(L0_attn_Wq, dtype=np.float32))
    wk = np.ascontiguousarray(np.asarray(L0_attn_Wk, dtype=np.float32))
    wi = np.ascontiguousarray(np.asarray(L0_attn_Wi, dtype=np.float32))
    wo = np.ascontiguousarray(np.asarray(L0_attn_Wo, dtype=np.float32))
    fw = np.ascontiguousarray(np.asarray(L0_ffn_weight, dtype=np.float32))
    fb = np.asarray(L0_ffn_bias, dtype=np.float32)
    gw = np.asarray(L0_attn_g_weight, dtype=np.float32)

    fbias = np.ascontiguousarray(fb.reshape(NCH, 128).T)
    gwin = np.ascontiguousarray(gw.reshape(NCH, 128).T)
    ident = np.eye(128, dtype=np.float32)
    causal = np.triu(np.ones((128, 128), dtype=np.uint8))
    onesc = np.ones((128, 128), dtype=np.float32)

    nc = build_program()
    in_maps = []
    for ci in range(NCORES):
        bidx, j = ci // 4, ci % 4
        t0 = j * TCORE
        if j == 0:
            xs = np.concatenate([np.zeros((HALO, D), np.float32), x[bidx, 0:TCORE]], axis=0)
            hm = np.zeros((128, 2), np.float32)
        else:
            xs = x[bidx, t0 - HALO:t0 + TCORE]
            hm = np.ones((128, 2), np.float32)
        in_maps.append({
            "x_fm": np.ascontiguousarray(xs.T),
            "wq": wq, "wk": wk, "wi": wi, "wo": wo, "fw": fw,
            "fbias": fbias, "gwin": gwin, "ident_in": ident,
            "causal_in": causal, "halom_in": hm, "onesc_in": onesc,
        })
    res = run_bass_kernel_spmd(nc, in_maps, core_ids=list(range(NCORES)))
    out = np.empty((B, L, D), dtype=np.float32)
    for ci in range(NCORES):
        bidx, j = ci // 4, ci % 4
        out[bidx, j * TCORE:(j + 1) * TCORE, :] = res.results[ci]["y_fm"].T
    if _DBG:
        kernel.dbg = [{k: res.results[ci]["dbg_" + k] for k in _DBG}
                      for ci in range(NCORES)]
    return out


# version 02
# speedup vs baseline: 1.0873x; 1.0873x over previous
import numpy as np
import concourse.bass as bass
import concourse.mybir as mybir
from concourse.bass_utils import run_bass_kernel_spmd

F32 = mybir.dt.float32
BF16 = mybir.dt.bfloat16
U8 = mybir.dt.uint8
A = mybir.AluOpType
AF = mybir.ActivationFunctionType
AX = mybir.AxisListType

D = 2048
NCH = D // 128
NH = 16
B = 2
L = 8192
NCORES = 8
HALO = 256
TCORE = 2048
TTOT = HALO + TCORE
TT = 256
NTILES = TTOT // TT
C = 128
SCALE = 128.0 ** -0.5
EPS_LN = 1e-5
EPS_RMS_FFN = 1e-8
EPS_FFN = 1e-8
EPS_ATTN = 1e-5
QB = 127.0
LADDER_BITS = [128.0, 64.0, 32.0, 16.0, 8.0, 4.0, 2.0, 1.0]
WT_BITS = [4.0, 2.0, 1.0]

_DBG = []

XR = [f"xr{c}" for c in range(NCH)]
XN = [f"xn{c}" for c in range(NCH)]
HX = [f"hx{c}" for c in range(NCH)]
LO = [f"lo{c}" for c in range(NCH)]
KR = [f"kr{c}" for c in range(NCH)]
VB = [f"vb{c}" for c in range(NCH)]
SK = [f"S{h}" for h in range(NH)]
QH = [f"qh{j}" for j in range(8)]
PB = [f"pb{b}" for b in range(8)]


class Plan:
    def __init__(self):
        self.ops = []
        self.writer = {}
        self.readers = {}

    def op(self, engine, fn, reads=(), writes=(), dma=False, narrow=False):
        deps = set()
        for bb in reads:
            w = self.writer.get(bb)
            if w is not None:
                deps.add(w)
        for bb in writes:
            w = self.writer.get(bb)
            if w is not None:
                deps.add(w)
            for r in self.readers.get(bb, ()):
                deps.add(r)
        i = len(self.ops)
        self.ops.append(dict(engine=engine, fn=fn, deps=deps, dma=dma,
                             narrow=narrow))
        for bb in reads:
            self.readers.setdefault(bb, []).append(i)
        for bb in writes:
            self.writer[bb] = i
            self.readers[bb] = []
        return i

    def emit(self, nc, sems):
        ops = self.ops
        engseq = [0] * len(ops)
        cnt = {}
        for i, o in enumerate(ops):
            engseq[i] = cnt.get(o["engine"], 0)
            cnt[o["engine"]] = engseq[i] + 1
        interlock = [set() for _ in ops]
        needs_inc = [False] * len(ops)
        for i, o in enumerate(ops):
            for d in o["deps"]:
                if ops[d]["dma"] or ops[d]["engine"] != o["engine"]:
                    needs_inc[d] = True
                elif ((ops[d]["narrow"] or o["narrow"])
                      and engseq[i] - engseq[d] <= 3):
                    needs_inc[d] = True
                    interlock[i].add(d)
        for i, o in enumerate(ops):
            if o["dma"]:
                needs_inc[i] = True
        last_of = {}
        for i, o in enumerate(ops):
            last_of[o["engine"]] = i
        for i in last_of.values():
            needs_inc[i] = True
        cum = {}
        opcum = [0] * len(ops)
        for i, o in enumerate(ops):
            eng = o["engine"]
            amt = 16 if o["dma"] else 1
            if needs_inc[i]:
                cum[eng] = cum.get(eng, 0) + amt
            opcum[i] = cum.get(eng, 0)
        hiwater = {}

        def emit_engine(eng_name, eng):
            hw = hiwater.setdefault(eng_name, {})
            for i, o in enumerate(ops):
                if o["engine"] != eng_name:
                    continue
                for d in sorted(o["deps"]):
                    do = ops[d]
                    if (do["engine"] == eng_name and not do["dma"]
                            and d not in interlock[i]):
                        continue
                    sem = sems[do["engine"]]
                    val = opcum[d]
                    if hw.get(do["engine"], 0) < val:
                        eng.wait_ge(sem, val)
                        hw[do["engine"]] = val
                ins = o["fn"](eng)
                if needs_inc[i]:
                    ins.then_inc(sems[eng_name], 16 if o["dma"] else 1)
        return emit_engine


def _ladder_floor(p, v_, u0, uu, u2, bits, keys, pre_add, eng="vector"):
    kv, k0, ku, k2 = keys
    p.op(eng, lambda e, a=v_, o=u0, c=pre_add: e.tensor_scalar(o, a, c, None, A.add),
         reads=[kv], writes=[k0])
    first = True
    for bit in bits:
        src = u0 if first else uu
        ksrc = k0 if first else ku
        p.op(eng, lambda e, s=src, o=u2, bb=bit: e.tensor_scalar(o, s, bb, -bb, A.is_ge, A.mult),
             reads=[ksrc], writes=[k2])
        p.op(eng, lambda e, s=src, o=uu, c=u2: e.tensor_tensor(o, s, c, A.add),
             reads=[ksrc, k2], writes=[ku])
        first = False
    p.op(eng, lambda e, o=u2, x=u0, y=uu: e.tensor_tensor(o, x, y, A.subtract),
         reads=[k0, ku], writes=[k2])


def build_program():
    nc = bass.Bass("TRN2", target_bir_lowering=False)

    x_fm = nc.dram_tensor("x_fm", [D, TTOT], F32, kind="ExternalInput").ap()
    wq = nc.dram_tensor("wq", [D, D], F32, kind="ExternalInput").ap()
    wk = nc.dram_tensor("wk", [D, D], F32, kind="ExternalInput").ap()
    wi = nc.dram_tensor("wi", [D, D], F32, kind="ExternalInput").ap()
    wo = nc.dram_tensor("wo", [D, D], F32, kind="ExternalInput").ap()
    fw = nc.dram_tensor("fw", [D, D], F32, kind="ExternalInput").ap()
    fbias = nc.dram_tensor("fbias", [128, NCH], F32, kind="ExternalInput").ap()
    gwin = nc.dram_tensor("gwin", [128, NCH], F32, kind="ExternalInput").ap()
    ident_in = nc.dram_tensor("ident_in", [128, 128], F32, kind="ExternalInput").ap()
    causal_in = nc.dram_tensor("causal_in", [128, 128], U8, kind="ExternalInput").ap()
    halom_in = nc.dram_tensor("halom_in", [128, 2], F32, kind="ExternalInput").ap()
    onesc_in = nc.dram_tensor("onesc_in", [128, 128], F32, kind="ExternalInput").ap()
    y_fm = nc.dram_tensor("y_fm", [D, TCORE], F32, kind="ExternalOutput").ap()
    wscr = [nc.dram_tensor(f"wscr{i}", [2 * D, D], BF16, kind="Internal").ap()
            for i in range(4)]
    dbg_t = {}
    for name in _DBG:
        dbg_t[name] = nc.dram_tensor("dbg_" + name, [128, 4096], F32,
                                     kind="ExternalOutput").ap()

    from contextlib import ExitStack
    es = ExitStack()

    def sb(name, shape, dt=F32):
        return es.enter_context(nc.sbuf_tensor(name, shape, dt))

    xres = sb("xres", [128, NCH * TT])
    xno = sb("xno", [128, NCH * TT])
    vbuf = sb("vbuf", [128, NCH * TT])
    kraw = sb("kraw", [128, NCH * TT])
    hixq = sb("hixq", [128, NCH * TT], BF16)
    lobf = sb("lobf", [128, NCH * TT], BF16)
    wtT = sb("wtT", [128, NCH * D], BF16)
    wsl = [sb(f"wsl{i}", [128, D], BF16) for i in range(2)]
    qhalf = sb("qhalf", [128, 8 * TT])
    kh = [sb(f"kh{i}", [128, TT]) for i in range(2)]
    gh = [sb(f"gh{i}", [128, TT]) for i in range(2)]
    th = [sb(f"th{i}", [128, TT]) for i in range(2)]
    bt = [sb(f"bt{i}", [128, C]) for i in range(2)]
    e1 = [sb(f"e1{i}", [128, C]) for i in range(2)]
    e2 = [sb(f"e2{i}", [128, C]) for i in range(2)]
    qint = [sb(f"qint{i}", [128, C]) for i in range(2)]
    qitr = [sb(f"qitr{i}", [128, C]) for i in range(2)]
    kot = [sb(f"kot{i}", [128, C]) for i in range(2)]
    Pp = [sb(f"Pp{i}", [128, C]) for i in range(2)]
    kTs = [sb(f"kTs{i}", [128, C]) for i in range(2)]
    vTs = [sb(f"vTs{i}", [128, C]) for i in range(2)]
    st8 = [sb(f"st8{i}", [128, 8]) for i in range(2)]
    stE = [sb(f"stE{i}", [128, 4]) for i in range(2)]
    Sst = sb("Sst", [128, NH * 128])
    sq = sb("sq", [128, 2 * TT])
    u0 = sb("u0", [128, TT])
    uu = sb("uu", [128, TT])
    u2 = sb("u2", [128, TT])
    rows = sb("rows", [1, 6 * TT])
    rowx = sb("rowx", [1, 2 * TT])
    colsb = sb("colsb", [128, 16])
    rb = sb("rb", [128, TT])
    ab = sb("ab", [128, TT])
    xsb = sb("xsb", [128, TT])
    invsb = sb("invsb", [128, TT])
    gamma = sb("gamma", [128, NCH])
    hbias = sb("hbias", [128, NCH])
    gwb = sb("gwb", [128, NCH])
    ident = sb("ident", [128, 128])
    causal = sb("causal", [128, 128], U8)
    halom = sb("halom", [128, 2])
    onesc = sb("onesc", [128, 128])
    Mmax = sb("Mmax", [128, TT])
    Mmin = sb("Mmin", [128, TT])
    ev = sb("ev", [128, 2 * TT])
    ev2 = sb("ev2", [128, 2 * TT])

    PS = es.enter_context(nc.psum_tensor("PS", [128, 4096], F32))
    ps_bk = [PS[:, j * 512:j * 512 + TT] for j in range(8)]
    ps_A = PS[:, 0:128]
    ps_O = [PS[:, 512:640], PS[:, 640:768]]
    ps_T = [PS[:, 1024:1152], PS[:, 1152:1280]]
    ps_S = [PS[:, 1536:1664], PS[:, 1664:1792]]
    ps_bc1 = PS[:, 4 * 512:4 * 512 + TT]
    ps_bc2 = PS[:, 5 * 512:5 * 512 + TT]
    ps_tr = PS[:, 6 * 512:6 * 512 + 128]
    ps_rr = PS[0:1, 6 * 512 + 128:6 * 512 + 256]
    ps_row2 = PS[0:1, 6 * 512 + 256:6 * 512 + 512]
    ps_row1 = PS[0:1, 7 * 512 + 256:7 * 512 + 512]

    sems = {}
    for s in ["sync", "tensor", "vector", "scalar", "gpsimd"]:
        sems[s] = es.enter_context(nc.semaphore("sem_" + s))

    p = Plan()

    def ch(buf, c, w=TT):
        return buf[:, c * w:(c + 1) * w]

    ROWMAP = {0: 0, 1: 1, 2: 2, 3: 3, 4: 4, 5: 5, 6: 4, 7: 5}

    def row(i):
        s = ROWMAP[i]
        return rows[0:1, s * TT:(s + 1) * TT]

    for dst, src, key in [(ident, ident_in, "ident"), (causal, causal_in, "causal"),
                          (halom, halom_in, "halom"), (onesc, onesc_in, "onesc"),
                          (gwb, gwin, "gwb"), (hbias, fbias, "hbias_raw")]:
        p.op("sync", lambda e, d=dst, s=src: e.dma_start(out=d[:, :], in_=s),
             writes=[key], dma=True)
    p.op("vector", lambda e: e.tensor_scalar(hbias[:, :], hbias[:, :], 0.5, None, A.mult),
         reads=["hbias_raw"], writes=["hbias"], narrow=True)
    for i in range(2):
        p.op("vector", lambda e, i=i: e.memset(Pp[i][:, :], 0.0), writes=[f"Ppz{i}"])
    p.op("vector", lambda e: e.memset(Sst[:, :], 0.0), writes=SK)

    ones_col = onesc[:, 0:1]
    ones_row = onesc[0:1, :]

    for widx, wsrc in enumerate([wq, wk, wi, wo]):
        for k in range(NCH):
            p.op("sync", lambda e, s=wsrc, k=k: e.dma_start(
                out=vbuf[:, 0:D], in_=s[k * 128:(k + 1) * 128, :]),
                reads=[], writes=["prep_w"], dma=True)
            p.op("vector", lambda e: e.tensor_copy(hixq[:, 0:D], vbuf[:, 0:D]),
                 reads=["prep_w"], writes=["prep_hi"])
            p.op("vector", lambda e: e.tensor_tensor(lobf[:, 0:D], vbuf[:, 0:D],
                                                     hixq[:, 0:D], A.subtract),
                 reads=["prep_w", "prep_hi"], writes=["prep_lo"])
            for nh in range(2):
                r0 = nh * D + k * 128
                p.op("sync", lambda e, r0=r0, w=widx, nh=nh: e.dma_start(
                    out=wscr[w][r0:r0 + 128, 0:1024],
                    in_=hixq[:, nh * 1024:(nh + 1) * 1024]),
                    reads=["prep_hi"], writes=[], dma=True)
                p.op("sync", lambda e, r0=r0, w=widx, nh=nh: e.dma_start(
                    out=wscr[w][r0:r0 + 128, 1024:2048],
                    in_=lobf[:, nh * 1024:(nh + 1) * 1024]),
                    reads=["prep_lo"], writes=[], dma=True)

    for n in range(NCH):
        p.op("sync", lambda e, n=n: e.dma_start(
            out=vbuf[:, 0:D], in_=fw[n * 128:(n + 1) * 128, :]),
            reads=["prep_hi", "prep_lo"], writes=["prep_w"], dma=True)
        p.op("vector", lambda e: e.reduce_sum(colsb[:, 1:2], vbuf[:, 0:D], AX.X,
                                              apply_absolute_value=True),
             reads=["prep_w"], writes=["c1"], narrow=True)
        p.op("vector", lambda e, n=n: e.tensor_scalar(gamma[:, n:n + 1], colsb[:, 1:2],
                                                      1.0 / 2048.0, None, A.mult),
             reads=["c1"], writes=["gamma"], narrow=True)
        p.op("vector", lambda e, n=n: e.tensor_scalar(colsb[:, 2:3], gamma[:, n:n + 1],
                                                      EPS_FFN, None, A.add),
             reads=["gamma"], writes=["c2"], narrow=True)
        p.op("vector", lambda e: e.reciprocal(colsb[:, 0:1], colsb[:, 2:3]),
             reads=["c2"], writes=["c0"], narrow=True)
        p.op("vector", lambda e: e.tensor_scalar(xno[:, 0:D], vbuf[:, 0:D],
                                                 colsb[:, 0:1], None, A.mult),
             reads=["prep_w", "c0"], writes=["prep_z"])
        for piece in range(D // TT):
            sl = slice(piece * TT, (piece + 1) * TT)
            _ladder_floor(p, xno[:, sl], u0[:, :], uu[:, :], u2[:, :],
                          WT_BITS, ("prep_z", "l0", "l1", "l2"), 2.5)
            p.op("vector", lambda e, sl=sl: e.tensor_scalar(kraw[:, sl], u2[:, :],
                                                            -2.0, -1.0, A.add, A.max),
                 reads=["l2"], writes=["prep_wt"])
            p.op("vector", lambda e, sl=sl: e.tensor_scalar(kraw[:, sl], kraw[:, sl],
                                                            1.0, None, A.min),
                 reads=["prep_wt"], writes=["prep_wt"])
        for j in range(NCH):
            p.op("tensor", lambda e, j=j: e.transpose(ps_tr, kraw[:, j * 128:(j + 1) * 128],
                                                      ident[:, :]),
                 reads=["prep_wt", "ident"], writes=["pb6"])
            p.op("scalar", lambda e, j=j, n=n: e.copy(
                out=wtT[:, j * D + n * 128:j * D + (n + 1) * 128], in_=ps_tr),
                reads=["pb6"], writes=["wtT"])

    p.op("sync", lambda e: e.nop(),
         reads=["prep_w", "prep_hi", "prep_lo", "prep_z", "prep_wt",
                "l0", "l1", "l2"],
         writes=XR + XN + HX + LO + KR + VB + ["lq", "l0", "l1", "l2"])

    def ln_stats(srcbuf, srckeys, with_rms):
        for c in range(NCH):
            slot = c % 2
            p.op("scalar", lambda e, c=c, s=slot: e.activation(
                out=sq[:, s * TT:(s + 1) * TT], in_=ch(srcbuf, c), func=AF.Square),
                reads=[srckeys[c]], writes=[f"sq{slot}"])
            p.op("tensor", lambda e, c=c: e.matmul(
                ps_row1, ones_col, ch(srcbuf, c), start=(c == 0), stop=(c == NCH - 1)),
                reads=[srckeys[c], "onesc"], writes=["pb7"])
            p.op("tensor", lambda e, c=c, s=slot: e.matmul(
                ps_row2, ones_col, sq[:, s * TT:(s + 1) * TT],
                start=(c == 0), stop=(c == NCH - 1)),
                reads=[f"sq{slot}", "onesc"], writes=["pb6"])
        p.op("vector", lambda e: e.tensor_scalar(row(0), ps_row1, 1.0 / 2048.0, None, A.mult),
             reads=["pb7"], writes=["r0"], narrow=True)
        p.op("vector", lambda e: e.tensor_scalar(row(1), ps_row2, 1.0 / 2048.0, None, A.mult),
             reads=["pb6"], writes=["r1"], narrow=True)
        p.op("vector", lambda e: e.tensor_tensor(row(6), row(0), row(0), A.mult),
             reads=["r0"], writes=["r4"], narrow=True)
        p.op("vector", lambda e: e.tensor_tensor(row(2), row(1), row(6), A.subtract),
             reads=["r1", "r4"], writes=["r2"], narrow=True)
        p.op("vector", lambda e: e.tensor_scalar(row(6), row(2), EPS_LN, None, A.add),
             reads=["r2"], writes=["r4"], narrow=True)
        p.op("scalar", lambda e: e.activation(out=row(7), in_=row(6), func=AF.Ln),
             reads=["r4"], writes=["r5"])
        p.op("vector", lambda e: e.tensor_scalar(row(7), row(7), -0.5, None, A.mult),
             reads=["r5"], writes=["r5"], narrow=True)
        p.op("scalar", lambda e: e.activation(out=row(3), in_=row(7), func=AF.Exp),
             reads=["r5"], writes=["r3"])
        if with_rms:
            p.op("vector", lambda e: e.tensor_tensor(row(6), row(3), row(3), A.mult),
                 reads=["r3"], writes=["r4"], narrow=True)
            p.op("vector", lambda e: e.tensor_tensor(row(6), row(2), row(6), A.mult),
                 reads=["r2", "r4"], writes=["r4"], narrow=True)
            p.op("vector", lambda e: e.tensor_scalar(row(6), row(6), EPS_RMS_FFN, None, A.add),
                 reads=["r4"], writes=["r4"], narrow=True)
            p.op("scalar", lambda e: e.activation(out=row(7), in_=row(6), func=AF.Ln),
                 reads=["r4"], writes=["r5"])
            p.op("vector", lambda e: e.tensor_scalar(row(7), row(7), -0.5, None, A.mult),
                 reads=["r5"], writes=["r5"], narrow=True)
            p.op("scalar", lambda e: e.activation(out=row(6), in_=row(7), func=AF.Exp),
                 reads=["r5"], writes=["r4"])
            p.op("vector", lambda e: e.tensor_tensor(row(4), row(3), row(6), A.mult),
                 reads=["r3", "r4"], writes=["r4"], narrow=True)
        else:
            p.op("vector", lambda e: e.tensor_copy(row(4), row(3)),
                 reads=["r3"], writes=["r4"], narrow=True)
        p.op("vector", lambda e: e.tensor_tensor(row(5), row(0), row(4), A.mult),
             reads=["r0", "r4"], writes=["r5"], narrow=True)
        p.op("tensor", lambda e: e.matmul(ps_bc1, ones_row, row(4), start=True, stop=True),
             reads=["r4", "onesc"], writes=["pb4"])
        p.op("tensor", lambda e: e.matmul(ps_bc2, ones_row, row(5), start=True, stop=True),
             reads=["r5", "onesc"], writes=["pb5"])
        p.op("scalar", lambda e: e.copy(out=rb[:, :], in_=ps_bc1),
             reads=["pb4"], writes=["rb"])
        p.op("scalar", lambda e: e.copy(out=ab[:, :], in_=ps_bc2),
             reads=["pb5"], writes=["ab"])

    def ln_apply(srcbuf, srckeys, dstbuf, dstkeys):
        for c in range(NCH):
            eng = "gpsimd" if c % 2 == 0 else "vector"
            p.op(eng, lambda e, c=c: e.tensor_tensor(ch(dstbuf, c), ch(srcbuf, c),
                                                     rb[:, :], A.mult),
                 reads=[srckeys[c], "rb"], writes=[dstkeys[c]])
            p.op(eng, lambda e, c=c: e.tensor_tensor(ch(dstbuf, c), ch(dstbuf, c),
                                                     ab[:, :], A.subtract),
                 reads=[dstkeys[c], "ab"], writes=[dstkeys[c]])

    def absmax_quant(srcbuf, srckeys):
        p.op("vector", lambda e: e.tensor_tensor(Mmax[:, :], ch(srcbuf, 0), ch(srcbuf, 1), A.max),
             reads=[srckeys[0], srckeys[1]], writes=["Mmax"])
        p.op("gpsimd", lambda e: e.tensor_copy(Mmin[:, :], ch(srcbuf, 0)),
             reads=[srckeys[0]], writes=["Mmin"])
        for c in range(1, NCH):
            if c > 1:
                p.op("vector", lambda e, c=c: e.tensor_tensor(Mmax[:, :], Mmax[:, :],
                                                              ch(srcbuf, c), A.max),
                     reads=[srckeys[c], "Mmax"], writes=["Mmax"])
            p.op("vector", lambda e, c=c: e.tensor_tensor(Mmin[:, :], Mmin[:, :],
                                                          ch(srcbuf, c), A.min),
                 reads=[srckeys[c], "Mmin"], writes=["Mmin"])
        for j in range(TT // 128):
            p.op("tensor", lambda e, j=j: e.transpose(ps_tr, Mmax[:, j * 128:(j + 1) * 128],
                                                      ident[:, :]),
                 reads=["Mmax", "ident"], writes=["pb6"])
            p.op("vector", lambda e, j=j: e.reduce_max(colsb[:, j:j + 1], ps_tr, AX.X),
                 reads=["pb6"], writes=["cmax"], narrow=True)
            p.op("tensor", lambda e, j=j: e.transpose(ps_tr, Mmin[:, j * 128:(j + 1) * 128],
                                                      ident[:, :]),
                 reads=["Mmin", "ident"], writes=["pb6"])
            p.op("vector", lambda e, j=j: e.tensor_reduce(colsb[:, 2 + j:3 + j], ps_tr, AX.X, A.min),
                 reads=["pb6"], writes=["cmin"], narrow=True)
        for j in range(TT // 128):
            p.op("vector", lambda e, j=j: e.tensor_scalar(colsb[:, 4 + j:5 + j],
                                                          colsb[:, 2 + j:3 + j], -1.0, None, A.mult),
                 reads=["cmin"], writes=["cneg"], narrow=True)
            p.op("vector", lambda e, j=j: e.tensor_tensor(colsb[:, 6 + j:7 + j],
                                                          colsb[:, j:j + 1],
                                                          colsb[:, 4 + j:5 + j], A.max),
                 reads=["cmax", "cneg"], writes=["cam"], narrow=True)
            p.op("vector", lambda e, j=j: e.tensor_scalar(colsb[:, 8 + j:9 + j],
                                                          colsb[:, 6 + j:7 + j],
                                                          1.0 / QB, None, A.mult),
                 reads=["cam"], writes=["cxs"], narrow=True)
            p.op("vector", lambda e, j=j: e.tensor_scalar(colsb[:, 10 + j:11 + j],
                                                          colsb[:, 8 + j:9 + j],
                                                          EPS_FFN, None, A.add),
                 reads=["cxs"], writes=["cpe"], narrow=True)
            p.op("vector", lambda e, j=j: e.reciprocal(
                colsb[:, 12 + j:13 + j], colsb[:, 10 + j:11 + j]),
                reads=["cpe"], writes=["cinv"], narrow=True)
        for j in range(TT // 128):
            p.op("tensor", lambda e, j=j: e.transpose(ps_rr, colsb[:, 8 + j:9 + j], ident[:, :]),
                 reads=["cxs", "ident"], writes=["pb6"])
            p.op("scalar", lambda e, j=j: e.copy(out=rowx[0:1, j * 128:(j + 1) * 128], in_=ps_rr),
                 reads=["pb6"], writes=["rowxs"])
            p.op("tensor", lambda e, j=j: e.transpose(ps_rr, colsb[:, 12 + j:13 + j], ident[:, :]),
                 reads=["cinv", "ident"], writes=["pb6"])
            p.op("scalar", lambda e, j=j: e.copy(out=rowx[0:1, TT + j * 128:TT + (j + 1) * 128],
                                                 in_=ps_rr),
                 reads=["pb6"], writes=["rowinv"])
        p.op("tensor", lambda e: e.matmul(ps_bc1, ones_row, rowx[0:1, 0:TT], start=True, stop=True),
             reads=["rowxs", "onesc"], writes=["pb4"])
        p.op("scalar", lambda e: e.copy(out=xsb[:, :], in_=ps_bc1),
             reads=["pb4"], writes=["xsb"])
        p.op("tensor", lambda e: e.matmul(ps_bc2, ones_row, rowx[0:1, TT:2 * TT], start=True, stop=True),
             reads=["rowinv", "onesc"], writes=["pb5"])
        p.op("scalar", lambda e: e.copy(out=invsb[:, :], in_=ps_bc2),
             reads=["pb5"], writes=["invsb"])
        for c in range(NCH):
            p.op("vector", lambda e, c=c: e.tensor_tensor(uu[:, :], ch(srcbuf, c),
                                                          invsb[:, :], A.mult),
                 reads=[srckeys[c], "invsb"], writes=["lq"])
            _ladder_floor(p, uu[:, :], u0[:, :], uu[:, :], u2[:, :],
                          LADDER_BITS, ("lq", "l0", "lq", "l2"), 128.5)
            p.op("vector", lambda e, c=c: e.tensor_scalar(u2[:, :], u2[:, :],
                                                          -128.0, -127.0, A.add, A.max),
                 reads=["l2"], writes=["l2"])
            p.op("vector", lambda e, c=c: e.tensor_scalar(ch(hixq, c), u2[:, :],
                                                          127.0, None, A.min),
                 reads=["l2"], writes=[HX[c]])

    def ffn_matmul_evac():
        for nh in range(2):
            for k in range(NCH):
                for j in range(8):
                    n = nh * 8 + j
                    p.op("tensor", lambda e, k=k, n=n, j=j: e.matmul(
                        ps_bk[j], wtT[:, k * D + n * 128:k * D + (n + 1) * 128],
                        ch(hixq, k), start=(k == 0), stop=(k == NCH - 1)),
                        reads=[HX[k], "wtT"], writes=[PB[j]])
            for j in range(8):
                n = nh * 8 + j
                slot = n % 2
                p.op("scalar", lambda e, n=n, j=j, s=slot: e.mul(
                    ev[:, s * TT:(s + 1) * TT], ps_bk[j], gamma[:, n:n + 1]),
                    reads=[PB[j], "gamma"], writes=[f"ev{slot}"])
                p.op("vector", lambda e, n=n, s=slot: e.tensor_tensor(
                    ev2[:, s * TT:(s + 1) * TT], ev[:, s * TT:(s + 1) * TT], xsb[:, :], A.mult),
                    reads=[f"ev{slot}", "xsb"], writes=[f"e2{slot}"])
                p.op("vector", lambda e, n=n, s=slot: e.tensor_scalar(
                    ev2[:, s * TT:(s + 1) * TT], ev2[:, s * TT:(s + 1) * TT],
                    0.5, hbias[:, n:n + 1], A.mult, A.add),
                    reads=[f"e2{slot}", "hbias"], writes=[f"e2{slot}"])
                p.op("vector", lambda e, n=n, s=slot: e.tensor_tensor(
                    ch(xres, n), ch(xres, n), ev2[:, s * TT:(s + 1) * TT], A.add),
                    reads=[f"e2{slot}", XR[n]], writes=[XR[n]])

    def stream_proj(widx, nhalf, evac):
        for k in range(NCH):
            slot = k % 2
            wkey = f"wsl{slot}"
            r0 = nhalf * D + k * 128
            p.op("sync", lambda e, r0=r0, s=slot, w=widx: e.dma_start(
                out=wsl[s][:, :], in_=wscr[w][r0:r0 + 128, :]),
                reads=[wkey + "_r"], writes=[wkey], dma=True)
            for j in range(8):
                whi = wsl[slot][:, j * 128:(j + 1) * 128]
                wlo = wsl[slot][:, 1024 + j * 128:1024 + (j + 1) * 128]
                xh = ch(hixq, k)
                xl = ch(lobf, k)
                p.op("tensor", lambda e, o=ps_bk[j], a=whi, bb=xh, f=(k == 0): e.matmul(
                    o, a, bb, start=f, stop=False),
                    reads=[wkey, HX[k]], writes=[PB[j]])
                p.op("tensor", lambda e, o=ps_bk[j], a=whi, bb=xl: e.matmul(
                    o, a, bb, start=False, stop=False),
                    reads=[wkey, LO[k]], writes=[PB[j]])
                p.op("tensor", lambda e, o=ps_bk[j], a=wlo, bb=xh,
                     last=(k == NCH - 1): e.matmul(o, a, bb, start=False, stop=last),
                     reads=[wkey, HX[k]], writes=[PB[j]])
            p.op("tensor", lambda e: e.nop(), reads=[], writes=[wkey + "_r"])
        for j in range(8):
            evac(nhalf * 8 + j, ps_bk[j], PB[j])

    def gla_head(h, has_q, tile_idx, dbg_taps=False):
        i = h % 2
        KHk, GHk, THk = f"kh{i}", f"gh{i}", f"th{i}"
        BTk, E1k, E2k = f"bt{i}", f"e1{i}", f"e2{i}"
        QIk, QTk, KOk = f"qint{i}", f"qitr{i}", f"qotr{i}"
        KOk = f"kot{i}"
        PPk, KTk, VTk = f"Pp{i}", f"kTs{i}", f"vTs{i}"
        S8k, SEk = f"st8{i}", f"stE{i}"
        pO, pT, pS = ps_O[i], ps_T[i], ps_S[i]
        bO, bT, bS = "pb1", "pb2", "pb3"
        p.op("scalar", lambda e, h=h: e.activation(out=th[i][:, :], in_=ch(kraw, h), func=AF.Exp),
             reads=[KR[h]], writes=[THk])
        p.op("vector", lambda e: e.tensor_scalar(th[i][:, :], th[i][:, :], 1.0, None, A.add),
             reads=[THk], writes=[THk])
        p.op("scalar", lambda e: e.activation(out=gh[i][:, :], in_=th[i][:, :], func=AF.Ln),
             reads=[THk], writes=[GHk])
        p.op("vector", lambda e, h=h: e.tensor_tensor(th[i][:, :], ch(kraw, h), gh[i][:, :], A.subtract),
             reads=[KR[h], GHk], writes=[THk])
        p.op("scalar", lambda e: e.activation(out=kh[i][:, :], in_=th[i][:, :], func=AF.Exp),
             reads=[THk], writes=[KHk])
        p.op("vector", lambda e: e.tensor_scalar(gh[i][:, :], gh[i][:, :], -1.0, None, A.mult),
             reads=[GHk], writes=[GHk])
        Sh = Sst[:, h * 128:(h + 1) * 128]
        for c in (0, 1):
            csl = slice(c * C, (c + 1) * C)
            p.op("tensor", lambda e, h=h, c=c: e.transpose(
                pT, vbuf[:, h * TT + c * C:h * TT + (c + 1) * C], ident[:, :]),
                reads=[VB[h], "ident", bT], writes=[bT])
            p.op("scalar", lambda e: e.copy(out=vTs[i][:, :], in_=pT),
                 reads=[bT], writes=[VTk])
            if tile_idx == 0:
                p.op("vector", lambda e, c=c: e.tensor_scalar(
                    vTs[i][:, :], vTs[i][:, :], halom[:, c:c + 1], None, A.mult),
                    reads=[VTk, "halom"], writes=[VTk])
            p.op("vector", lambda e, csl=csl: e.tensor_tensor_scan(
                bt[i][:, :], gh[i][:, csl], gh[i][:, csl], 0.0, A.add, A.bypass),
                reads=[GHk], writes=[BTk])
            p.op("vector", lambda e: e.tensor_copy(st8[i][:, 0:1], bt[i][:, C - 1:C]),
                 reads=[BTk], writes=[S8k], narrow=True)
            p.op("vector", lambda e: e.tensor_scalar(st8[i][:, 1:2], st8[i][:, 0:1], 0.5, None, A.mult),
                 reads=[S8k], writes=[S8k], narrow=True)
            p.op("scalar", lambda e: e.activation(out=stE[i][:, 0:2], in_=st8[i][:, 0:2], func=AF.Exp),
                 reads=[S8k], writes=[SEk])
            p.op("scalar", lambda e: e.activation(out=e1[i][:, :], in_=bt[i][:, :], func=AF.Exp),
                 reads=[BTk], writes=[E1k])
            if has_q:
                qsl = qhalf[:, (h % 8) * TT + c * C:(h % 8) * TT + (c + 1) * C]
                p.op("vector", lambda e, q=qsl: e.tensor_tensor(qint[i][:, :], q, e1[i][:, :], A.mult),
                     reads=[QH[h % 8], E1k], writes=[QIk])
            p.op("vector", lambda e: e.tensor_scalar(bt[i][:, :], bt[i][:, :], -1.0,
                                                     st8[i][:, 1:2], A.mult, A.add),
                 reads=[BTk, S8k], writes=[BTk], narrow=True)
            p.op("vector", lambda e: e.tensor_scalar(bt[i][:, :], bt[i][:, :], 80.0, None, A.min),
                 reads=[BTk], writes=[BTk])
            p.op("scalar", lambda e: e.activation(out=e2[i][:, :], in_=bt[i][:, :], func=AF.Exp),
                 reads=[BTk], writes=[E2k])
            if has_q:
                p.op("scalar", lambda e: e.activation(out=e1[i][:, :], in_=bt[i][:, :],
                                                      func=AF.Exp, scale=-1.0),
                     reads=[BTk], writes=[E1k])
                p.op("vector", lambda e, q=qsl: e.tensor_tensor(qitr[i][:, :], q, e1[i][:, :], A.mult),
                     reads=[QH[h % 8], E1k], writes=[QTk])
            p.op("vector", lambda e, csl=csl: e.tensor_tensor(kot[i][:, :], kh[i][:, csl],
                                                              e2[i][:, :], A.mult),
                 reads=[KHk, E2k], writes=[KOk])
            if has_q:
                p.op("tensor", lambda e: e.matmul(ps_A, kot[i][:, :], qitr[i][:, :],
                                                  start=True, stop=True),
                     reads=[KOk, QTk, "pb0"], writes=["pb0"])
                p.op("vector", lambda e: e.copy_predicated(Pp[i][:, :], causal[:, :], ps_A),
                     reads=["pb0", "causal", f"Ppz{i}"], writes=[PPk])
                p.op("tensor", lambda e: e.matmul(pO, vTs[i][:, :], Pp[i][:, :],
                                                  start=True, stop=False),
                     reads=[VTk, PPk, bO], writes=[bO])
                p.op("tensor", lambda e, S=Sh, hh=h: e.matmul(pO, S, qint[i][:, :],
                                                              start=False, stop=True),
                     reads=[SK[h], QIk], writes=[bO])
                p.op("scalar", lambda e, h=h, c=c: e.copy(
                    out=xno[:, h * TT + c * C:h * TT + (c + 1) * C], in_=pO),
                    reads=[bO], writes=[XN[h]])
            p.op("vector", lambda e: e.tensor_scalar(kot[i][:, :], kot[i][:, :],
                                                     stE[i][:, 1:2], None, A.mult),
                 reads=[KOk, SEk], writes=[KOk])
            p.op("tensor", lambda e: e.transpose(pT, kot[i][:, :], ident[:, :]),
                 reads=[KOk, "ident", bT], writes=[bT])
            p.op("scalar", lambda e: e.copy(out=kTs[i][:, :], in_=pT),
                 reads=[bT], writes=[KTk])
            p.op("tensor", lambda e: e.matmul(pS, kTs[i][:, :], vTs[i][:, :],
                                              start=True, stop=True),
                 reads=[KTk, VTk, bS], writes=[bS])
            p.op("vector", lambda e, S=Sh: e.tensor_scalar(S, S, stE[i][:, 0:1], None, A.mult),
                 reads=[SK[h], SEk], writes=[SK[h]])
            p.op("vector", lambda e, S=Sh: e.tensor_tensor(S, S, pS, A.add),
                 reads=[SK[h], bS], writes=[SK[h]])

    def dbg_dump(name, buf, keys):
        if name in dbg_t:
            w = buf[:, :].shape[1]
            p.op("sync", lambda e, d=dbg_t[name], s=buf, w=w: e.dma_start(
                out=d[:, 0:w], in_=s[:, :]),
                reads=list(keys), writes=[], dma=True)

    for tile in range(NTILES):
        halo_tile = (tile == 0)
        src = x_fm.rearrange("(c p) t -> p c t", p=128)[:, :, tile * TT:(tile + 1) * TT]
        dst = xres[:, :].rearrange("p (c t) -> p c t", t=TT)
        p.op("sync", lambda e, s=src, d=dst: e.dma_start(out=d, in_=s),
             reads=XR, writes=XR, dma=True)

        if tile == 1:
            dbg_dump("x0", xres, XR)
        ln_stats(xres, XR, with_rms=True)
        ln_apply(xres, XR, xno, XN)
        if tile == 1:
            dbg_dump("xn1", xno, XN)
        absmax_quant(xno, XN)
        ffn_matmul_evac()
        if tile == 1:
            dbg_dump("x1", xres, XR)

        ln_stats(xres, XR, with_rms=False)
        ln_apply(xres, XR, xno, XN)
        for c in range(NCH):
            eng = "gpsimd" if c % 2 == 0 else "vector"
            p.op(eng, lambda e, c=c: e.tensor_copy(ch(hixq, c), ch(xno, c)),
                 reads=[XN[c]], writes=[HX[c]])
            p.op(eng, lambda e, c=c: e.tensor_tensor(ch(lobf, c), ch(xno, c),
                                                     ch(hixq, c), A.subtract),
                 reads=[XN[c], HX[c]], writes=[LO[c]])

        def evac_v(n, ps, pbk):
            p.op("scalar", lambda e, n=n, ps=ps: e.copy(out=ch(vbuf, n), in_=ps),
                 reads=[pbk], writes=[VB[n]])
        for nh in range(2):
            stream_proj(2, nh, evac_v)
        if tile == 1:
            dbg_dump("v1", vbuf, VB)

        def evac_k(n, ps, pbk):
            p.op("scalar", lambda e, n=n, ps=ps: e.copy(out=ch(kraw, n), in_=ps),
                 reads=[pbk], writes=[KR[n]])
        for nh in range(2):
            stream_proj(1, nh, evac_k)
        if tile == 1:
            dbg_dump("kraw1", kraw, KR)

        if halo_tile:
            for h in range(NH):
                gla_head(h, has_q=False, tile_idx=tile)
            dbg_dump("S0", Sst, SK)
        else:
            for half in range(2):
                def evac_q(n, ps, pbk):
                    p.op("scalar", lambda e, n=n, ps=ps: e.activation(
                        out=qhalf[:, (n % 8) * TT:((n % 8) + 1) * TT], in_=ps,
                        func=AF.Relu, scale=SCALE),
                        reads=[pbk], writes=[QH[n % 8]])
                stream_proj(0, half, evac_q)
                for h in range(8 * half, 8 * half + 8):
                    gla_head(h, has_q=True, tile_idx=tile)
            if tile == 1:
                dbg_dump("o1", xno, XN)

            for c in range(NCH):
                slot = c % 2
                p.op("scalar", lambda e, c=c, s=slot: e.activation(
                    out=sq[:, s * TT:(s + 1) * TT], in_=ch(xno, c), func=AF.Square),
                    reads=[XN[c]], writes=[f"sq{slot}"])
                p.op("tensor", lambda e, c=c, s=slot: e.matmul(
                    ps_row2, ones_col, sq[:, s * TT:(s + 1) * TT],
                    start=(c == 0), stop=(c == NCH - 1)),
                    reads=[f"sq{slot}", "onesc"], writes=["pb6"])
            p.op("vector", lambda e: e.tensor_scalar(row(6), ps_row2, 1.0 / 2048.0, None, A.mult),
                 reads=["pb6"], writes=["r4"], narrow=True)
            p.op("vector", lambda e: e.tensor_scalar(row(6), row(6), EPS_ATTN, None, A.add),
                 reads=["r4"], writes=["r4"], narrow=True)
            p.op("scalar", lambda e: e.activation(out=row(7), in_=row(6), func=AF.Ln),
                 reads=["r4"], writes=["r5"])
            p.op("vector", lambda e: e.tensor_scalar(row(7), row(7), -0.5, None, A.mult),
                 reads=["r5"], writes=["r5"], narrow=True)
            p.op("scalar", lambda e: e.activation(out=row(6), in_=row(7), func=AF.Exp),
                 reads=["r5"], writes=["r4"])
            p.op("tensor", lambda e: e.matmul(ps_bc1, ones_row, row(6), start=True, stop=True),
                 reads=["r4", "onesc"], writes=["pb4"])
            p.op("scalar", lambda e: e.copy(out=rb[:, :], in_=ps_bc1),
                 reads=["pb4"], writes=["rb"])
            for c in range(NCH):
                slot = c % 2
                p.op("vector", lambda e, c=c, s=slot: e.tensor_tensor(
                    ev[:, s * TT:(s + 1) * TT], ch(xno, c), rb[:, :], A.mult),
                    reads=[XN[c], "rb"], writes=[f"ev{slot}"])
                p.op("vector", lambda e, c=c, s=slot: e.tensor_scalar(
                    ev2[:, s * TT:(s + 1) * TT], ev[:, s * TT:(s + 1) * TT],
                    gwb[:, c:c + 1], None, A.mult),
                    reads=[f"ev{slot}", "gwb"], writes=[f"e2{slot}"])
                p.op("vector", lambda e, c=c, s=slot: e.tensor_copy(
                    ch(hixq, c), ev2[:, s * TT:(s + 1) * TT]),
                    reads=[f"e2{slot}"], writes=[HX[c]])
                p.op("vector", lambda e, c=c, s=slot: e.tensor_tensor(
                    ch(lobf, c), ev2[:, s * TT:(s + 1) * TT], ch(hixq, c), A.subtract),
                    reads=[f"e2{slot}", HX[c]], writes=[LO[c]])

            def evac_wo(n, ps, pbk):
                p.op("vector", lambda e, n=n, ps=ps: e.tensor_tensor(
                    ch(xres, n), ch(xres, n), ps, A.add),
                    reads=[pbk, XR[n]], writes=[XR[n]])
            for nh in range(2):
                stream_proj(3, nh, evac_wo)
            if tile == 1:
                dbg_dump("x2", xres, XR)

            ln_stats(xres, XR, with_rms=True)
            ln_apply(xres, XR, xno, XN)
            absmax_quant(xno, XN)
            ffn_matmul_evac()

            osrc = xres[:, :].rearrange("p (c t) -> p c t", t=TT)
            odst = y_fm.rearrange("(c p) t -> p c t", p=128)[
                :, :, (tile - 1) * TT:tile * TT]
            p.op("sync", lambda e, s=osrc, d=odst: e.dma_start(out=d, in_=s),
                 reads=XR + ["outdma"], writes=["outdma"], dma=True)

    p.op("sync", lambda e: e.nop(),
         reads=["outdma"] + XR + XN + SK + ["wtT"], writes=[])

    emit_engine = p.emit(nc, sems)
    with nc.Block() as blk:
        blk.sync(lambda e: emit_engine("sync", e))
        blk.tensor(lambda e: emit_engine("tensor", e))
        blk.vector(lambda e: emit_engine("vector", e))
        blk.scalar(lambda e: emit_engine("scalar", e))
        blk.gpsimd(lambda e: emit_engine("gpsimd", e))
    es.close()
    return nc


def kernel(L0_attn_hidden_states, L0_attn_Wq, L0_attn_Wk, L0_attn_Wi, L0_attn_Wo,
           L0_attn_g_weight, L0_ffn_weight, L0_ffn_bias):
    x = np.asarray(L0_attn_hidden_states, dtype=np.float32)
    wq = np.ascontiguousarray(np.asarray(L0_attn_Wq, dtype=np.float32))
    wk = np.ascontiguousarray(np.asarray(L0_attn_Wk, dtype=np.float32))
    wi = np.ascontiguousarray(np.asarray(L0_attn_Wi, dtype=np.float32))
    wo = np.ascontiguousarray(np.asarray(L0_attn_Wo, dtype=np.float32))
    fw = np.ascontiguousarray(np.asarray(L0_ffn_weight, dtype=np.float32))
    fb = np.asarray(L0_ffn_bias, dtype=np.float32)
    gw = np.asarray(L0_attn_g_weight, dtype=np.float32)

    fbias = np.ascontiguousarray(fb.reshape(NCH, 128).T)
    gwin = np.ascontiguousarray(gw.reshape(NCH, 128).T)
    ident = np.eye(128, dtype=np.float32)
    causal = np.triu(np.ones((128, 128), dtype=np.uint8))
    onesc = np.ones((128, 128), dtype=np.float32)

    nc = build_program()
    in_maps = []
    for ci in range(NCORES):
        bidx, j = ci // 4, ci % 4
        t0 = j * TCORE
        if j == 0:
            xs = np.concatenate([np.zeros((HALO, D), np.float32), x[bidx, 0:TCORE]], axis=0)
            hm = np.zeros((128, 2), np.float32)
        else:
            xs = x[bidx, t0 - HALO:t0 + TCORE]
            hm = np.ones((128, 2), np.float32)
        in_maps.append({
            "x_fm": np.ascontiguousarray(xs.T),
            "wq": wq, "wk": wk, "wi": wi, "wo": wo, "fw": fw,
            "fbias": fbias, "gwin": gwin, "ident_in": ident,
            "causal_in": causal, "halom_in": hm, "onesc_in": onesc,
        })
    res = run_bass_kernel_spmd(nc, in_maps, core_ids=list(range(NCORES)))
    out = np.empty((B, L, D), dtype=np.float32)
    for ci in range(NCORES):
        bidx, j = ci // 4, ci % 4
        out[bidx, j * TCORE:(j + 1) * TCORE, :] = res.results[ci]["y_fm"].T
    if _DBG:
        kernel.dbg = [{k: res.results[ci]["dbg_" + k] for k in _DBG}
                      for ci in range(NCORES)]
    return out


# version 03
# speedup vs baseline: 1.1060x; 1.0172x over previous
import numpy as np
import concourse.bass as bass
import concourse.mybir as mybir
from concourse.bass_utils import run_bass_kernel_spmd

F32 = mybir.dt.float32
BF16 = mybir.dt.bfloat16
U8 = mybir.dt.uint8
A = mybir.AluOpType
AF = mybir.ActivationFunctionType
AX = mybir.AxisListType

D = 2048
NCH = D // 128
NH = 16
B = 2
L = 8192
NCORES = 8
HALO = 256
TCORE = 2048
TTOT = HALO + TCORE
TT = 256
NTILES = TTOT // TT
C = 128
SCALE = 128.0 ** -0.5
EPS_LN = 1e-5
EPS_RMS_FFN = 1e-8
EPS_FFN = 1e-8
EPS_ATTN = 1e-5
QB = 127.0
LADDER_BITS = [128.0, 64.0, 32.0, 16.0, 8.0, 4.0, 2.0, 1.0]
WT_BITS = [4.0, 2.0, 1.0]

_DBG = []

XR = [f"xr{c}" for c in range(NCH)]
XN = [f"xn{c}" for c in range(NCH)]
HX = [f"hx{c}" for c in range(NCH)]
LO = [f"lo{c}" for c in range(NCH)]
KR = [f"kr{c}" for c in range(NCH)]
VB = [f"vb{c}" for c in range(NCH)]
SK = [f"S{h}" for h in range(NH)]
QH = [f"qh{j}" for j in range(8)]
PB = [f"pb{b}" for b in range(8)]


class Plan:
    def __init__(self):
        self.ops = []
        self.writer = {}
        self.readers = {}

    def op(self, engine, fn, reads=(), writes=(), dma=False, narrow=False):
        deps = set()
        for bb in reads:
            w = self.writer.get(bb)
            if w is not None:
                deps.add(w)
        for bb in writes:
            w = self.writer.get(bb)
            if w is not None:
                deps.add(w)
            for r in self.readers.get(bb, ()):
                deps.add(r)
        i = len(self.ops)
        self.ops.append(dict(engine=engine, fn=fn, deps=deps, dma=dma,
                             narrow=narrow))
        for bb in reads:
            self.readers.setdefault(bb, []).append(i)
        for bb in writes:
            self.writer[bb] = i
            self.readers[bb] = []
        return i

    def emit(self, nc, sems):
        ops = self.ops
        engseq = [0] * len(ops)
        cnt = {}
        for i, o in enumerate(ops):
            engseq[i] = cnt.get(o["engine"], 0)
            cnt[o["engine"]] = engseq[i] + 1
        interlock = [set() for _ in ops]
        needs_inc = [False] * len(ops)
        for i, o in enumerate(ops):
            for d in o["deps"]:
                if ops[d]["dma"] or ops[d]["engine"] != o["engine"]:
                    needs_inc[d] = True
                elif ((ops[d]["narrow"] or o["narrow"])
                      and engseq[i] - engseq[d] <= 3):
                    needs_inc[d] = True
                    interlock[i].add(d)
        for i, o in enumerate(ops):
            if o["dma"]:
                needs_inc[i] = True
        last_of = {}
        for i, o in enumerate(ops):
            last_of[o["engine"]] = i
        for i in last_of.values():
            needs_inc[i] = True
        cum = {}
        opcum = [0] * len(ops)
        for i, o in enumerate(ops):
            eng = o["engine"]
            amt = 16 if o["dma"] else 1
            if needs_inc[i]:
                cum[eng] = cum.get(eng, 0) + amt
            opcum[i] = cum.get(eng, 0)
        hiwater = {}

        def emit_engine(eng_name, eng):
            hw = hiwater.setdefault(eng_name, {})
            for i, o in enumerate(ops):
                if o["engine"] != eng_name:
                    continue
                for d in sorted(o["deps"]):
                    do = ops[d]
                    if (do["engine"] == eng_name and not do["dma"]
                            and d not in interlock[i]):
                        continue
                    sem = sems[do["engine"]]
                    val = opcum[d]
                    if hw.get(do["engine"], 0) < val:
                        eng.wait_ge(sem, val)
                        hw[do["engine"]] = val
                ins = o["fn"](eng)
                if needs_inc[i]:
                    ins.then_inc(sems[eng_name], 16 if o["dma"] else 1)
        return emit_engine


def _ladder_floor(p, v_, u0, uu, u2, bits, keys, pre_add, eng="vector",
                  sign_levels=0):
    kv, k0, ku, k2 = keys
    p.op(eng, lambda e, a=v_, o=u0, c=pre_add: e.tensor_scalar(o, a, c, None, A.add),
         reads=[kv], writes=[k0])
    D = 0.0
    first = True
    for li, bit in enumerate(bits):
        src = u0 if first else uu
        ksrc = k0 if first else ku
        if li < sign_levels:
            assert bit + D == 128.0
            p.op("scalar", lambda e, s=src, bb=bit + D: e.activation(
                out=u2, in_=s, func=AF.Sign, bias=-bb),
                reads=[ksrc, "mconst"], writes=[k2])
            p.op(eng, lambda e, s=src, o=uu, c=u2, hb=bit / 2.0: e.scalar_tensor_tensor(
                o, c, -hb, s, A.mult, A.add),
                reads=[ksrc, k2], writes=[ku])
            D += bit / 2.0
        else:
            p.op(eng, lambda e, s=src, o=u2, bb=bit, dd=D: e.tensor_scalar(
                o, s, bb + dd, -bb, A.is_ge, A.mult),
                reads=[ksrc], writes=[k2])
            p.op(eng, lambda e, s=src, o=uu, c=u2: e.tensor_tensor(o, s, c, A.add),
                 reads=[ksrc, k2], writes=[ku])
        first = False
    p.op(eng, lambda e, o=u2, x=u0, y=uu: e.tensor_tensor(o, x, y, A.subtract),
         reads=[k0, ku], writes=[k2])
    return D


def build_program():
    nc = bass.Bass("TRN2", target_bir_lowering=False)

    x_fm = nc.dram_tensor("x_fm", [D, TTOT], F32, kind="ExternalInput").ap()
    wq = nc.dram_tensor("wq", [D, D], F32, kind="ExternalInput").ap()
    wk = nc.dram_tensor("wk", [D, D], F32, kind="ExternalInput").ap()
    wi = nc.dram_tensor("wi", [D, D], F32, kind="ExternalInput").ap()
    wo = nc.dram_tensor("wo", [D, D], F32, kind="ExternalInput").ap()
    fw = nc.dram_tensor("fw", [D, D], F32, kind="ExternalInput").ap()
    fbias = nc.dram_tensor("fbias", [128, NCH], F32, kind="ExternalInput").ap()
    gwin = nc.dram_tensor("gwin", [128, NCH], F32, kind="ExternalInput").ap()
    ident_in = nc.dram_tensor("ident_in", [128, 128], F32, kind="ExternalInput").ap()
    causal_in = nc.dram_tensor("causal_in", [128, 128], U8, kind="ExternalInput").ap()
    halom_in = nc.dram_tensor("halom_in", [128, 2], F32, kind="ExternalInput").ap()
    onesc_in = nc.dram_tensor("onesc_in", [128, 128], F32, kind="ExternalInput").ap()
    mconst_in = nc.dram_tensor("mconst_in", [128, 2], F32, kind="ExternalInput").ap()
    y_fm = nc.dram_tensor("y_fm", [D, TCORE], F32, kind="ExternalOutput").ap()
    wscr = [nc.dram_tensor(f"wscr{i}", [2 * D, D], BF16, kind="Internal").ap()
            for i in range(4)]
    dbg_t = {}
    for name in _DBG:
        dbg_t[name] = nc.dram_tensor("dbg_" + name, [128, 4096], F32,
                                     kind="ExternalOutput").ap()

    from contextlib import ExitStack
    es = ExitStack()

    def sb(name, shape, dt=F32):
        return es.enter_context(nc.sbuf_tensor(name, shape, dt))

    xres = sb("xres", [128, NCH * TT])
    xno = sb("xno", [128, NCH * TT])
    vbuf = sb("vbuf", [128, NCH * TT])
    kraw = sb("kraw", [128, NCH * TT])
    hixq = sb("hixq", [128, NCH * TT], BF16)
    lobf = sb("lobf", [128, NCH * TT], BF16)
    wtT = sb("wtT", [128, NCH * D], BF16)
    wsl = [sb(f"wsl{i}", [128, D], BF16) for i in range(2)]
    qhalf = sb("qhalf", [128, 8 * TT])
    kh = [sb(f"kh{i}", [128, TT]) for i in range(2)]
    gh = [sb(f"gh{i}", [128, TT]) for i in range(2)]
    th = [sb(f"th{i}", [128, TT]) for i in range(2)]
    bt = [sb(f"bt{i}", [128, C]) for i in range(2)]
    e1 = [sb(f"e1{i}", [128, C]) for i in range(2)]
    e2 = [sb(f"e2{i}", [128, C]) for i in range(2)]
    qint = [sb(f"qint{i}", [128, C]) for i in range(2)]
    qitr = [sb(f"qitr{i}", [128, C]) for i in range(2)]
    kot = [sb(f"kot{i}", [128, C]) for i in range(2)]
    Pp = [sb(f"Pp{i}", [128, C]) for i in range(2)]
    kTs = [sb(f"kTs{i}", [128, C]) for i in range(2)]
    vTs = [sb(f"vTs{i}", [128, C]) for i in range(2)]
    st8 = [sb(f"st8{i}", [128, 8]) for i in range(2)]
    stE = [sb(f"stE{i}", [128, 4]) for i in range(2)]
    Sst = sb("Sst", [128, NH * 128])
    sq = sb("sq", [128, 2 * TT])
    u0 = sb("u0", [128, TT])
    uu = sb("uu", [128, TT])
    u2 = sb("u2", [128, TT])
    rows = sb("rows", [1, 6 * TT])
    rowx = sb("rowx", [1, 2 * TT])
    colsb = sb("colsb", [128, 16])
    rb = sb("rb", [128, TT])
    ab = sb("ab", [128, TT])
    xsb = sb("xsb", [128, TT])
    invsb = sb("invsb", [128, TT])
    gamma = sb("gamma", [128, NCH])
    hbias = sb("hbias", [128, NCH])
    gwb = sb("gwb", [128, NCH])
    ident = sb("ident", [128, 128])
    causal = sb("causal", [128, 128], U8)
    halom = sb("halom", [128, 2])
    onesc = sb("onesc", [128, 128])
    mconst = sb("mconst", [128, 2])
    Mmax = sb("Mmax", [128, TT])
    Mmin = sb("Mmin", [128, TT])
    ev = sb("ev", [128, 2 * TT])
    ev2 = sb("ev2", [128, 2 * TT])

    PS = es.enter_context(nc.psum_tensor("PS", [128, 4096], F32))
    ps_bk = [PS[:, j * 512:j * 512 + TT] for j in range(8)]
    ps_A = PS[:, 0:128]
    ps_O = [PS[:, 512:640], PS[:, 640:768]]
    ps_T = [PS[:, 1024:1152], PS[:, 1152:1280]]
    ps_S = [PS[:, 1536:1664], PS[:, 1664:1792]]
    ps_bc1 = PS[:, 4 * 512:4 * 512 + TT]
    ps_bc2 = PS[:, 5 * 512:5 * 512 + TT]
    ps_tr = PS[:, 6 * 512:6 * 512 + 128]
    ps_rr = PS[0:1, 6 * 512 + 128:6 * 512 + 256]
    ps_row2 = PS[0:1, 6 * 512 + 256:6 * 512 + 512]
    ps_row1 = PS[0:1, 7 * 512 + 256:7 * 512 + 512]

    nc.const_aps.register(F32, -128.0, mconst[:, 0:1])
    sems = {}
    for s in ["sync", "tensor", "vector", "scalar", "gpsimd"]:
        sems[s] = es.enter_context(nc.semaphore("sem_" + s))

    p = Plan()

    def ch(buf, c, w=TT):
        return buf[:, c * w:(c + 1) * w]

    ROWMAP = {0: 0, 1: 1, 2: 2, 3: 3, 4: 4, 5: 5, 6: 4, 7: 5}

    def row(i):
        s = ROWMAP[i]
        return rows[0:1, s * TT:(s + 1) * TT]

    for dst, src, key in [(ident, ident_in, "ident"), (causal, causal_in, "causal"),
                          (halom, halom_in, "halom"), (onesc, onesc_in, "onesc"),
                          (gwb, gwin, "gwb"), (hbias, fbias, "hbias_raw"),
                          (mconst, mconst_in, "mconst")]:
        p.op("sync", lambda e, d=dst, s=src: e.dma_start(out=d[:, :], in_=s),
             writes=[key], dma=True)
    p.op("vector", lambda e: e.tensor_scalar(hbias[:, :], hbias[:, :], 0.5, None, A.mult),
         reads=["hbias_raw"], writes=["hbias"], narrow=True)
    for i in range(2):
        p.op("vector", lambda e, i=i: e.memset(Pp[i][:, :], 0.0), writes=[f"Ppz{i}"])
    p.op("vector", lambda e: e.memset(Sst[:, :], 0.0), writes=SK)

    ones_col = onesc[:, 0:1]
    ones_row = onesc[0:1, :]

    for widx, wsrc in enumerate([wq, wk, wi, wo]):
        for k in range(NCH):
            p.op("sync", lambda e, s=wsrc, k=k: e.dma_start(
                out=vbuf[:, 0:D], in_=s[k * 128:(k + 1) * 128, :]),
                reads=[], writes=["prep_w"], dma=True)
            p.op("vector", lambda e: e.tensor_copy(hixq[:, 0:D], vbuf[:, 0:D]),
                 reads=["prep_w"], writes=["prep_hi"])
            p.op("vector", lambda e: e.tensor_tensor(lobf[:, 0:D], vbuf[:, 0:D],
                                                     hixq[:, 0:D], A.subtract),
                 reads=["prep_w", "prep_hi"], writes=["prep_lo"])
            for nh in range(2):
                r0 = nh * D + k * 128
                p.op("sync", lambda e, r0=r0, w=widx, nh=nh: e.dma_start(
                    out=wscr[w][r0:r0 + 128, 0:1024],
                    in_=hixq[:, nh * 1024:(nh + 1) * 1024]),
                    reads=["prep_hi"], writes=[], dma=True)
                p.op("sync", lambda e, r0=r0, w=widx, nh=nh: e.dma_start(
                    out=wscr[w][r0:r0 + 128, 1024:2048],
                    in_=lobf[:, nh * 1024:(nh + 1) * 1024]),
                    reads=["prep_lo"], writes=[], dma=True)

    for n in range(NCH):
        p.op("sync", lambda e, n=n: e.dma_start(
            out=vbuf[:, 0:D], in_=fw[n * 128:(n + 1) * 128, :]),
            reads=["prep_hi", "prep_lo"], writes=["prep_w"], dma=True)
        p.op("vector", lambda e: e.reduce_sum(colsb[:, 1:2], vbuf[:, 0:D], AX.X,
                                              apply_absolute_value=True),
             reads=["prep_w"], writes=["c1"], narrow=True)
        p.op("vector", lambda e, n=n: e.tensor_scalar(gamma[:, n:n + 1], colsb[:, 1:2],
                                                      1.0 / 2048.0, None, A.mult),
             reads=["c1"], writes=["gamma"], narrow=True)
        p.op("vector", lambda e, n=n: e.tensor_scalar(colsb[:, 2:3], gamma[:, n:n + 1],
                                                      EPS_FFN, None, A.add),
             reads=["gamma"], writes=["c2"], narrow=True)
        p.op("vector", lambda e: e.reciprocal(colsb[:, 0:1], colsb[:, 2:3]),
             reads=["c2"], writes=["c0"], narrow=True)
        p.op("vector", lambda e: e.tensor_scalar(xno[:, 0:D], vbuf[:, 0:D],
                                                 colsb[:, 0:1], None, A.mult),
             reads=["prep_w", "c0"], writes=["prep_z"])
        for piece in range(D // TT):
            sl = slice(piece * TT, (piece + 1) * TT)
            _ladder_floor(p, xno[:, sl], u0[:, :], uu[:, :], u2[:, :],
                          WT_BITS, ("prep_z", "l0", "l1", "l2"), 2.5)
            p.op("vector", lambda e, sl=sl: e.tensor_scalar(kraw[:, sl], u2[:, :],
                                                            -2.0, -1.0, A.add, A.max),
                 reads=["l2"], writes=["prep_wt"])
            p.op("vector", lambda e, sl=sl: e.tensor_scalar(kraw[:, sl], kraw[:, sl],
                                                            1.0, None, A.min),
                 reads=["prep_wt"], writes=["prep_wt"])
        for j in range(NCH):
            p.op("tensor", lambda e, j=j: e.transpose(ps_tr, kraw[:, j * 128:(j + 1) * 128],
                                                      ident[:, :]),
                 reads=["prep_wt", "ident"], writes=["pb6"])
            p.op("scalar", lambda e, j=j, n=n: e.copy(
                out=wtT[:, j * D + n * 128:j * D + (n + 1) * 128], in_=ps_tr),
                reads=["pb6"], writes=["wtT"])

    p.op("sync", lambda e: e.nop(),
         reads=["prep_w", "prep_hi", "prep_lo", "prep_z", "prep_wt",
                "l0", "l1", "l2"],
         writes=XR + XN + HX + LO + KR + VB + ["lq", "l0", "l1", "l2"])

    def ln_stats(srcbuf, srckeys, with_rms):
        for c in range(NCH):
            slot = c % 2
            p.op("scalar", lambda e, c=c, s=slot: e.activation(
                out=sq[:, s * TT:(s + 1) * TT], in_=ch(srcbuf, c), func=AF.Square),
                reads=[srckeys[c]], writes=[f"sq{slot}"])
            p.op("tensor", lambda e, c=c: e.matmul(
                ps_row1, ones_col, ch(srcbuf, c), start=(c == 0), stop=(c == NCH - 1)),
                reads=[srckeys[c], "onesc"], writes=["pb7"])
            p.op("tensor", lambda e, c=c, s=slot: e.matmul(
                ps_row2, ones_col, sq[:, s * TT:(s + 1) * TT],
                start=(c == 0), stop=(c == NCH - 1)),
                reads=[f"sq{slot}", "onesc"], writes=["pb6"])
        p.op("vector", lambda e: e.tensor_scalar(row(0), ps_row1, 1.0 / 2048.0, None, A.mult),
             reads=["pb7"], writes=["r0"], narrow=True)
        p.op("vector", lambda e: e.tensor_scalar(row(1), ps_row2, 1.0 / 2048.0, None, A.mult),
             reads=["pb6"], writes=["r1"], narrow=True)
        p.op("vector", lambda e: e.tensor_tensor(row(6), row(0), row(0), A.mult),
             reads=["r0"], writes=["r4"], narrow=True)
        p.op("vector", lambda e: e.tensor_tensor(row(2), row(1), row(6), A.subtract),
             reads=["r1", "r4"], writes=["r2"], narrow=True)
        p.op("vector", lambda e: e.tensor_scalar(row(6), row(2), EPS_LN, None, A.add),
             reads=["r2"], writes=["r4"], narrow=True)
        p.op("scalar", lambda e: e.activation(out=row(7), in_=row(6), func=AF.Ln),
             reads=["r4"], writes=["r5"])
        p.op("vector", lambda e: e.tensor_scalar(row(7), row(7), -0.5, None, A.mult),
             reads=["r5"], writes=["r5"], narrow=True)
        p.op("scalar", lambda e: e.activation(out=row(3), in_=row(7), func=AF.Exp),
             reads=["r5"], writes=["r3"])
        if with_rms:
            p.op("vector", lambda e: e.tensor_tensor(row(6), row(3), row(3), A.mult),
                 reads=["r3"], writes=["r4"], narrow=True)
            p.op("vector", lambda e: e.tensor_tensor(row(6), row(2), row(6), A.mult),
                 reads=["r2", "r4"], writes=["r4"], narrow=True)
            p.op("vector", lambda e: e.tensor_scalar(row(6), row(6), EPS_RMS_FFN, None, A.add),
                 reads=["r4"], writes=["r4"], narrow=True)
            p.op("scalar", lambda e: e.activation(out=row(7), in_=row(6), func=AF.Ln),
                 reads=["r4"], writes=["r5"])
            p.op("vector", lambda e: e.tensor_scalar(row(7), row(7), -0.5, None, A.mult),
                 reads=["r5"], writes=["r5"], narrow=True)
            p.op("scalar", lambda e: e.activation(out=row(6), in_=row(7), func=AF.Exp),
                 reads=["r5"], writes=["r4"])
            p.op("vector", lambda e: e.tensor_tensor(row(4), row(3), row(6), A.mult),
                 reads=["r3", "r4"], writes=["r4"], narrow=True)
        else:
            p.op("vector", lambda e: e.tensor_copy(row(4), row(3)),
                 reads=["r3"], writes=["r4"], narrow=True)
        p.op("vector", lambda e: e.tensor_tensor(row(5), row(0), row(4), A.mult),
             reads=["r0", "r4"], writes=["r5"], narrow=True)
        p.op("tensor", lambda e: e.matmul(ps_bc1, ones_row, row(4), start=True, stop=True),
             reads=["r4", "onesc"], writes=["pb4"])
        p.op("tensor", lambda e: e.matmul(ps_bc2, ones_row, row(5), start=True, stop=True),
             reads=["r5", "onesc"], writes=["pb5"])
        p.op("scalar", lambda e: e.copy(out=rb[:, :], in_=ps_bc1),
             reads=["pb4"], writes=["rb"])
        p.op("scalar", lambda e: e.copy(out=ab[:, :], in_=ps_bc2),
             reads=["pb5"], writes=["ab"])

    def ln_apply(srcbuf, srckeys, dstbuf, dstkeys):
        for c in range(NCH):
            eng = "gpsimd" if c % 2 == 0 else "vector"
            p.op(eng, lambda e, c=c: e.tensor_tensor(ch(dstbuf, c), ch(srcbuf, c),
                                                     rb[:, :], A.mult),
                 reads=[srckeys[c], "rb"], writes=[dstkeys[c]])
            p.op(eng, lambda e, c=c: e.tensor_tensor(ch(dstbuf, c), ch(dstbuf, c),
                                                     ab[:, :], A.subtract),
                 reads=[dstkeys[c], "ab"], writes=[dstkeys[c]])

    def absmax_quant(srcbuf, srckeys):
        p.op("vector", lambda e: e.tensor_tensor(Mmax[:, :], ch(srcbuf, 0), ch(srcbuf, 1), A.max),
             reads=[srckeys[0], srckeys[1]], writes=["Mmax"])
        p.op("gpsimd", lambda e: e.tensor_copy(Mmin[:, :], ch(srcbuf, 0)),
             reads=[srckeys[0]], writes=["Mmin"])
        for c in range(1, NCH):
            if c > 1:
                p.op("vector", lambda e, c=c: e.tensor_tensor(Mmax[:, :], Mmax[:, :],
                                                              ch(srcbuf, c), A.max),
                     reads=[srckeys[c], "Mmax"], writes=["Mmax"])
            p.op("vector", lambda e, c=c: e.tensor_tensor(Mmin[:, :], Mmin[:, :],
                                                          ch(srcbuf, c), A.min),
                 reads=[srckeys[c], "Mmin"], writes=["Mmin"])
        for j in range(TT // 128):
            p.op("tensor", lambda e, j=j: e.transpose(ps_tr, Mmax[:, j * 128:(j + 1) * 128],
                                                      ident[:, :]),
                 reads=["Mmax", "ident"], writes=["pb6"])
            p.op("vector", lambda e, j=j: e.reduce_max(colsb[:, j:j + 1], ps_tr, AX.X),
                 reads=["pb6"], writes=["cmax"], narrow=True)
            p.op("tensor", lambda e, j=j: e.transpose(ps_tr, Mmin[:, j * 128:(j + 1) * 128],
                                                      ident[:, :]),
                 reads=["Mmin", "ident"], writes=["pb6"])
            p.op("vector", lambda e, j=j: e.tensor_reduce(colsb[:, 2 + j:3 + j], ps_tr, AX.X, A.min),
                 reads=["pb6"], writes=["cmin"], narrow=True)
        for j in range(TT // 128):
            p.op("vector", lambda e, j=j: e.tensor_scalar(colsb[:, 4 + j:5 + j],
                                                          colsb[:, 2 + j:3 + j], -1.0, None, A.mult),
                 reads=["cmin"], writes=["cneg"], narrow=True)
            p.op("vector", lambda e, j=j: e.tensor_tensor(colsb[:, 6 + j:7 + j],
                                                          colsb[:, j:j + 1],
                                                          colsb[:, 4 + j:5 + j], A.max),
                 reads=["cmax", "cneg"], writes=["cam"], narrow=True)
            p.op("vector", lambda e, j=j: e.tensor_scalar(colsb[:, 8 + j:9 + j],
                                                          colsb[:, 6 + j:7 + j],
                                                          1.0 / QB, None, A.mult),
                 reads=["cam"], writes=["cxs"], narrow=True)
            p.op("vector", lambda e, j=j: e.tensor_scalar(colsb[:, 10 + j:11 + j],
                                                          colsb[:, 8 + j:9 + j],
                                                          EPS_FFN, None, A.add),
                 reads=["cxs"], writes=["cpe"], narrow=True)
            p.op("vector", lambda e, j=j: e.reciprocal(
                colsb[:, 12 + j:13 + j], colsb[:, 10 + j:11 + j]),
                reads=["cpe"], writes=["cinv"], narrow=True)
        for j in range(TT // 128):
            p.op("tensor", lambda e, j=j: e.transpose(ps_rr, colsb[:, 8 + j:9 + j], ident[:, :]),
                 reads=["cxs", "ident"], writes=["pb6"])
            p.op("scalar", lambda e, j=j: e.copy(out=rowx[0:1, j * 128:(j + 1) * 128], in_=ps_rr),
                 reads=["pb6"], writes=["rowxs"])
            p.op("tensor", lambda e, j=j: e.transpose(ps_rr, colsb[:, 12 + j:13 + j], ident[:, :]),
                 reads=["cinv", "ident"], writes=["pb6"])
            p.op("scalar", lambda e, j=j: e.copy(out=rowx[0:1, TT + j * 128:TT + (j + 1) * 128],
                                                 in_=ps_rr),
                 reads=["pb6"], writes=["rowinv"])
        p.op("tensor", lambda e: e.matmul(ps_bc1, ones_row, rowx[0:1, 0:TT], start=True, stop=True),
             reads=["rowxs", "onesc"], writes=["pb4"])
        p.op("scalar", lambda e: e.copy(out=xsb[:, :], in_=ps_bc1),
             reads=["pb4"], writes=["xsb"])
        p.op("tensor", lambda e: e.matmul(ps_bc2, ones_row, rowx[0:1, TT:2 * TT], start=True, stop=True),
             reads=["rowinv", "onesc"], writes=["pb5"])
        p.op("scalar", lambda e: e.copy(out=invsb[:, :], in_=ps_bc2),
             reads=["pb5"], writes=["invsb"])
        for c in range(NCH):
            p.op("vector", lambda e, c=c: e.tensor_tensor(uu[:, :], ch(srcbuf, c),
                                                          invsb[:, :], A.mult),
                 reads=[srckeys[c], "invsb"], writes=["lq"])
            Dd = _ladder_floor(p, uu[:, :], u0[:, :], uu[:, :], u2[:, :],
                               LADDER_BITS, ("lq", "l0", "lq", "l2"), 128.5,
                               sign_levels=0)
            p.op("vector", lambda e, c=c, dd=Dd: e.tensor_scalar(
                u2[:, :], u2[:, :], dd - 128.0, -127.0, A.add, A.max),
                reads=["l2"], writes=["l2"])
            p.op("vector", lambda e, c=c: e.tensor_scalar(ch(hixq, c), u2[:, :],
                                                          127.0, None, A.min),
                 reads=["l2"], writes=[HX[c]])

    def ffn_matmul_evac():
        for nh in range(2):
            for k in range(NCH):
                for j in range(8):
                    n = nh * 8 + j
                    p.op("tensor", lambda e, k=k, n=n, j=j: e.matmul(
                        ps_bk[j], wtT[:, k * D + n * 128:k * D + (n + 1) * 128],
                        ch(hixq, k), start=(k == 0), stop=(k == NCH - 1)),
                        reads=[HX[k], "wtT"], writes=[PB[j]])
            for j in range(8):
                n = nh * 8 + j
                slot = n % 2
                p.op("scalar", lambda e, n=n, j=j, s=slot: e.mul(
                    ev[:, s * TT:(s + 1) * TT], ps_bk[j], gamma[:, n:n + 1]),
                    reads=[PB[j], "gamma"], writes=[f"ev{slot}"])
                p.op("vector", lambda e, n=n, s=slot: e.tensor_tensor(
                    ev2[:, s * TT:(s + 1) * TT], ev[:, s * TT:(s + 1) * TT], xsb[:, :], A.mult),
                    reads=[f"ev{slot}", "xsb"], writes=[f"e2{slot}"])
                p.op("vector", lambda e, n=n, s=slot: e.tensor_scalar(
                    ev2[:, s * TT:(s + 1) * TT], ev2[:, s * TT:(s + 1) * TT],
                    0.5, hbias[:, n:n + 1], A.mult, A.add),
                    reads=[f"e2{slot}", "hbias"], writes=[f"e2{slot}"])
                p.op("gpsimd" if n % 2 == 0 else "vector",
                     lambda e, n=n, s=slot: e.tensor_tensor(
                    ch(xres, n), ch(xres, n), ev2[:, s * TT:(s + 1) * TT], A.add),
                    reads=[f"e2{slot}", XR[n]], writes=[XR[n]])

    def stream_proj(widx, nhalf, evac):
        for k in range(NCH):
            slot = k % 2
            wkey = f"wsl{slot}"
            r0 = nhalf * D + k * 128
            p.op("sync", lambda e, r0=r0, s=slot, w=widx: e.dma_start(
                out=wsl[s][:, :], in_=wscr[w][r0:r0 + 128, :]),
                reads=[wkey + "_r"], writes=[wkey], dma=True)
            for j in range(8):
                whi = wsl[slot][:, j * 128:(j + 1) * 128]
                wlo = wsl[slot][:, 1024 + j * 128:1024 + (j + 1) * 128]
                xh = ch(hixq, k)
                xl = ch(lobf, k)
                p.op("tensor", lambda e, o=ps_bk[j], a=whi, bb=xh, f=(k == 0): e.matmul(
                    o, a, bb, start=f, stop=False),
                    reads=[wkey, HX[k]], writes=[PB[j]])
                p.op("tensor", lambda e, o=ps_bk[j], a=whi, bb=xl: e.matmul(
                    o, a, bb, start=False, stop=False),
                    reads=[wkey, LO[k]], writes=[PB[j]])
                p.op("tensor", lambda e, o=ps_bk[j], a=wlo, bb=xh,
                     last=(k == NCH - 1): e.matmul(o, a, bb, start=False, stop=last),
                     reads=[wkey, HX[k]], writes=[PB[j]])
            p.op("tensor", lambda e: e.nop(), reads=[], writes=[wkey + "_r"])
        for j in range(8):
            evac(nhalf * 8 + j, ps_bk[j], PB[j])

    def gla_head(h, has_q, tile_idx, dbg_taps=False):
        i = h % 2
        KHk, GHk, THk = f"kh{i}", f"gh{i}", f"th{i}"
        BTk, E1k, E2k = f"bt{i}", f"e1{i}", f"e2{i}"
        QIk, QTk, KOk = f"qint{i}", f"qitr{i}", f"qotr{i}"
        KOk = f"kot{i}"
        PPk, KTk, VTk = f"Pp{i}", f"kTs{i}", f"vTs{i}"
        S8k, SEk = f"st8{i}", f"stE{i}"
        pO, pT, pS = ps_O[i], ps_T[i], ps_S[i]
        bO, bT, bS = "pb1", "pb2", "pb3"
        p.op("scalar", lambda e, h=h: e.activation(out=th[i][:, :], in_=ch(kraw, h), func=AF.Exp),
             reads=[KR[h]], writes=[THk])
        p.op("vector", lambda e: e.tensor_scalar(th[i][:, :], th[i][:, :], 1.0, None, A.add),
             reads=[THk], writes=[THk])
        p.op("scalar", lambda e: e.activation(out=gh[i][:, :], in_=th[i][:, :], func=AF.Ln),
             reads=[THk], writes=[GHk])
        p.op("vector", lambda e, h=h: e.tensor_tensor(th[i][:, :], ch(kraw, h), gh[i][:, :], A.subtract),
             reads=[KR[h], GHk], writes=[THk])
        p.op("scalar", lambda e: e.activation(out=kh[i][:, :], in_=th[i][:, :], func=AF.Exp),
             reads=[THk], writes=[KHk])
        p.op("vector", lambda e: e.tensor_scalar(gh[i][:, :], gh[i][:, :], -1.0, None, A.mult),
             reads=[GHk], writes=[GHk])
        Sh = Sst[:, h * 128:(h + 1) * 128]
        for c in (0, 1):
            csl = slice(c * C, (c + 1) * C)
            p.op("tensor", lambda e, h=h, c=c: e.transpose(
                pT, vbuf[:, h * TT + c * C:h * TT + (c + 1) * C], ident[:, :]),
                reads=[VB[h], "ident", bT], writes=[bT])
            p.op("scalar", lambda e: e.copy(out=vTs[i][:, :], in_=pT),
                 reads=[bT], writes=[VTk])
            if tile_idx == 0:
                p.op("vector", lambda e, c=c: e.tensor_scalar(
                    vTs[i][:, :], vTs[i][:, :], halom[:, c:c + 1], None, A.mult),
                    reads=[VTk, "halom"], writes=[VTk])
            p.op("vector", lambda e, csl=csl: e.tensor_tensor_scan(
                bt[i][:, :], gh[i][:, csl], gh[i][:, csl], 0.0, A.add, A.bypass),
                reads=[GHk], writes=[BTk])
            p.op("vector", lambda e: e.tensor_scalar(st8[i][:, 1:2], bt[i][:, C - 1:C],
                                                     0.5, None, A.mult),
                 reads=[BTk], writes=[S8k], narrow=True)
            p.op("scalar", lambda e: e.activation(out=stE[i][:, 0:1], in_=bt[i][:, C - 1:C],
                                                  func=AF.Exp),
                 reads=[BTk], writes=[SEk])
            p.op("scalar", lambda e: e.activation(out=stE[i][:, 1:2], in_=bt[i][:, C - 1:C],
                                                  func=AF.Exp, scale=0.5),
                 reads=[BTk], writes=[SEk])
            p.op("scalar", lambda e: e.activation(out=e1[i][:, :], in_=bt[i][:, :], func=AF.Exp),
                 reads=[BTk], writes=[E1k])
            if has_q:
                qsl = qhalf[:, (h % 8) * TT + c * C:(h % 8) * TT + (c + 1) * C]
                p.op("vector", lambda e, q=qsl: e.tensor_tensor(qint[i][:, :], q, e1[i][:, :], A.mult),
                     reads=[QH[h % 8], E1k], writes=[QIk])
            p.op("vector", lambda e: e.tensor_scalar(bt[i][:, :], bt[i][:, :], -1.0,
                                                     st8[i][:, 1:2], A.mult, A.add),
                 reads=[BTk, S8k], writes=[BTk], narrow=True)
            p.op("vector", lambda e: e.tensor_scalar(bt[i][:, :], bt[i][:, :], 80.0, None, A.min),
                 reads=[BTk], writes=[BTk])
            p.op("scalar", lambda e: e.activation(out=e2[i][:, :], in_=bt[i][:, :], func=AF.Exp),
                 reads=[BTk], writes=[E2k])
            if has_q:
                p.op("scalar", lambda e: e.activation(out=e1[i][:, :], in_=bt[i][:, :],
                                                      func=AF.Exp, scale=-1.0),
                     reads=[BTk], writes=[E1k])
                p.op("vector", lambda e, q=qsl: e.tensor_tensor(qitr[i][:, :], q, e1[i][:, :], A.mult),
                     reads=[QH[h % 8], E1k], writes=[QTk])
            p.op("vector", lambda e, csl=csl: e.tensor_tensor(kot[i][:, :], kh[i][:, csl],
                                                              e2[i][:, :], A.mult),
                 reads=[KHk, E2k], writes=[KOk])
            if has_q:
                p.op("tensor", lambda e: e.matmul(ps_A, kot[i][:, :], qitr[i][:, :],
                                                  start=True, stop=True),
                     reads=[KOk, QTk, "pb0"], writes=["pb0"])
                p.op("vector", lambda e: e.copy_predicated(Pp[i][:, :], causal[:, :], ps_A),
                     reads=["pb0", "causal", f"Ppz{i}"], writes=[PPk])
                p.op("tensor", lambda e: e.matmul(pO, vTs[i][:, :], Pp[i][:, :],
                                                  start=True, stop=False),
                     reads=[VTk, PPk, bO], writes=[bO])
                p.op("tensor", lambda e, S=Sh, hh=h: e.matmul(pO, S, qint[i][:, :],
                                                              start=False, stop=True),
                     reads=[SK[h], QIk], writes=[bO])
                p.op("scalar", lambda e, h=h, c=c: e.copy(
                    out=xno[:, h * TT + c * C:h * TT + (c + 1) * C], in_=pO),
                    reads=[bO], writes=[XN[h]])
            p.op("vector", lambda e: e.tensor_scalar(kot[i][:, :], kot[i][:, :],
                                                     stE[i][:, 1:2], None, A.mult),
                 reads=[KOk, SEk], writes=[KOk])
            p.op("tensor", lambda e: e.transpose(pT, kot[i][:, :], ident[:, :]),
                 reads=[KOk, "ident", bT], writes=[bT])
            p.op("scalar", lambda e: e.copy(out=kTs[i][:, :], in_=pT),
                 reads=[bT], writes=[KTk])
            p.op("tensor", lambda e: e.matmul(pS, kTs[i][:, :], vTs[i][:, :],
                                              start=True, stop=True),
                 reads=[KTk, VTk, bS], writes=[bS])
            p.op("vector", lambda e, S=Sh: e.tensor_scalar(S, S, stE[i][:, 0:1], None, A.mult),
                 reads=[SK[h], SEk], writes=[SK[h]])
            p.op("vector", lambda e, S=Sh: e.tensor_tensor(S, S, pS, A.add),
                 reads=[SK[h], bS], writes=[SK[h]])

    def dbg_dump(name, buf, keys):
        if name in dbg_t:
            w = buf[:, :].shape[1]
            p.op("sync", lambda e, d=dbg_t[name], s=buf, w=w: e.dma_start(
                out=d[:, 0:w], in_=s[:, :]),
                reads=list(keys), writes=[], dma=True)

    for tile in range(NTILES):
        halo_tile = (tile == 0)
        src = x_fm.rearrange("(c p) t -> p c t", p=128)[:, :, tile * TT:(tile + 1) * TT]
        dst = xres[:, :].rearrange("p (c t) -> p c t", t=TT)
        p.op("sync", lambda e, s=src, d=dst: e.dma_start(out=d, in_=s),
             reads=XR, writes=XR, dma=True)

        if tile == 1:
            dbg_dump("x0", xres, XR)
        ln_stats(xres, XR, with_rms=True)
        ln_apply(xres, XR, xno, XN)
        if tile == 1:
            dbg_dump("xn1", xno, XN)
        absmax_quant(xno, XN)
        ffn_matmul_evac()
        if tile == 1:
            dbg_dump("x1", xres, XR)

        ln_stats(xres, XR, with_rms=False)
        ln_apply(xres, XR, xno, XN)
        for c in range(NCH):
            eng = "gpsimd" if c % 2 == 0 else "vector"
            p.op(eng, lambda e, c=c: e.tensor_copy(ch(hixq, c), ch(xno, c)),
                 reads=[XN[c]], writes=[HX[c]])
            p.op(eng, lambda e, c=c: e.tensor_tensor(ch(lobf, c), ch(xno, c),
                                                     ch(hixq, c), A.subtract),
                 reads=[XN[c], HX[c]], writes=[LO[c]])

        def evac_v(n, ps, pbk):
            p.op("scalar", lambda e, n=n, ps=ps: e.copy(out=ch(vbuf, n), in_=ps),
                 reads=[pbk], writes=[VB[n]])
        for nh in range(2):
            stream_proj(2, nh, evac_v)
        if tile == 1:
            dbg_dump("v1", vbuf, VB)

        def evac_k(n, ps, pbk):
            p.op("scalar", lambda e, n=n, ps=ps: e.copy(out=ch(kraw, n), in_=ps),
                 reads=[pbk], writes=[KR[n]])
        for nh in range(2):
            stream_proj(1, nh, evac_k)
        if tile == 1:
            dbg_dump("kraw1", kraw, KR)

        if halo_tile:
            for h in range(NH):
                gla_head(h, has_q=False, tile_idx=tile)
            dbg_dump("S0", Sst, SK)
        else:
            for half in range(2):
                def evac_q(n, ps, pbk):
                    p.op("scalar", lambda e, n=n, ps=ps: e.activation(
                        out=qhalf[:, (n % 8) * TT:((n % 8) + 1) * TT], in_=ps,
                        func=AF.Relu, scale=SCALE),
                        reads=[pbk], writes=[QH[n % 8]])
                stream_proj(0, half, evac_q)
                for h in range(8 * half, 8 * half + 8):
                    gla_head(h, has_q=True, tile_idx=tile)
            if tile == 1:
                dbg_dump("o1", xno, XN)

            for c in range(NCH):
                slot = c % 2
                p.op("scalar", lambda e, c=c, s=slot: e.activation(
                    out=sq[:, s * TT:(s + 1) * TT], in_=ch(xno, c), func=AF.Square),
                    reads=[XN[c]], writes=[f"sq{slot}"])
                p.op("tensor", lambda e, c=c, s=slot: e.matmul(
                    ps_row2, ones_col, sq[:, s * TT:(s + 1) * TT],
                    start=(c == 0), stop=(c == NCH - 1)),
                    reads=[f"sq{slot}", "onesc"], writes=["pb6"])
            p.op("vector", lambda e: e.tensor_scalar(row(6), ps_row2, 1.0 / 2048.0, None, A.mult),
                 reads=["pb6"], writes=["r4"], narrow=True)
            p.op("vector", lambda e: e.tensor_scalar(row(6), row(6), EPS_ATTN, None, A.add),
                 reads=["r4"], writes=["r4"], narrow=True)
            p.op("scalar", lambda e: e.activation(out=row(7), in_=row(6), func=AF.Ln),
                 reads=["r4"], writes=["r5"])
            p.op("vector", lambda e: e.tensor_scalar(row(7), row(7), -0.5, None, A.mult),
                 reads=["r5"], writes=["r5"], narrow=True)
            p.op("scalar", lambda e: e.activation(out=row(6), in_=row(7), func=AF.Exp),
                 reads=["r5"], writes=["r4"])
            p.op("tensor", lambda e: e.matmul(ps_bc1, ones_row, row(6), start=True, stop=True),
                 reads=["r4", "onesc"], writes=["pb4"])
            p.op("scalar", lambda e: e.copy(out=rb[:, :], in_=ps_bc1),
                 reads=["pb4"], writes=["rb"])
            for c in range(NCH):
                slot = c % 2
                p.op("vector", lambda e, c=c, s=slot: e.tensor_tensor(
                    ev[:, s * TT:(s + 1) * TT], ch(xno, c), rb[:, :], A.mult),
                    reads=[XN[c], "rb"], writes=[f"ev{slot}"])
                p.op("vector", lambda e, c=c, s=slot: e.tensor_scalar(
                    ev2[:, s * TT:(s + 1) * TT], ev[:, s * TT:(s + 1) * TT],
                    gwb[:, c:c + 1], None, A.mult),
                    reads=[f"ev{slot}", "gwb"], writes=[f"e2{slot}"])
                p.op("vector", lambda e, c=c, s=slot: e.tensor_copy(
                    ch(hixq, c), ev2[:, s * TT:(s + 1) * TT]),
                    reads=[f"e2{slot}"], writes=[HX[c]])
                p.op("vector", lambda e, c=c, s=slot: e.tensor_tensor(
                    ch(lobf, c), ev2[:, s * TT:(s + 1) * TT], ch(hixq, c), A.subtract),
                    reads=[f"e2{slot}", HX[c]], writes=[LO[c]])

            def evac_wo(n, ps, pbk):
                p.op("vector", lambda e, n=n, ps=ps: e.tensor_tensor(
                    ch(xres, n), ch(xres, n), ps, A.add),
                    reads=[pbk, XR[n]], writes=[XR[n]])
            for nh in range(2):
                stream_proj(3, nh, evac_wo)
            if tile == 1:
                dbg_dump("x2", xres, XR)

            ln_stats(xres, XR, with_rms=True)
            ln_apply(xres, XR, xno, XN)
            absmax_quant(xno, XN)
            ffn_matmul_evac()

            osrc = xres[:, :].rearrange("p (c t) -> p c t", t=TT)
            odst = y_fm.rearrange("(c p) t -> p c t", p=128)[
                :, :, (tile - 1) * TT:tile * TT]
            p.op("sync", lambda e, s=osrc, d=odst: e.dma_start(out=d, in_=s),
                 reads=XR + ["outdma"], writes=["outdma"], dma=True)

    p.op("sync", lambda e: e.nop(),
         reads=["outdma"] + XR + XN + SK + ["wtT"], writes=[])

    emit_engine = p.emit(nc, sems)
    with nc.Block() as blk:
        blk.sync(lambda e: emit_engine("sync", e))
        blk.tensor(lambda e: emit_engine("tensor", e))
        blk.vector(lambda e: emit_engine("vector", e))
        blk.scalar(lambda e: emit_engine("scalar", e))
        blk.gpsimd(lambda e: emit_engine("gpsimd", e))
    es.close()
    return nc


def kernel(L0_attn_hidden_states, L0_attn_Wq, L0_attn_Wk, L0_attn_Wi, L0_attn_Wo,
           L0_attn_g_weight, L0_ffn_weight, L0_ffn_bias):
    x = np.asarray(L0_attn_hidden_states, dtype=np.float32)
    wq = np.ascontiguousarray(np.asarray(L0_attn_Wq, dtype=np.float32))
    wk = np.ascontiguousarray(np.asarray(L0_attn_Wk, dtype=np.float32))
    wi = np.ascontiguousarray(np.asarray(L0_attn_Wi, dtype=np.float32))
    wo = np.ascontiguousarray(np.asarray(L0_attn_Wo, dtype=np.float32))
    fw = np.ascontiguousarray(np.asarray(L0_ffn_weight, dtype=np.float32))
    fb = np.asarray(L0_ffn_bias, dtype=np.float32)
    gw = np.asarray(L0_attn_g_weight, dtype=np.float32)

    fbias = np.ascontiguousarray(fb.reshape(NCH, 128).T)
    gwin = np.ascontiguousarray(gw.reshape(NCH, 128).T)
    ident = np.eye(128, dtype=np.float32)
    causal = np.triu(np.ones((128, 128), dtype=np.uint8))
    onesc = np.ones((128, 128), dtype=np.float32)
    mconst = np.zeros((128, 2), dtype=np.float32)
    mconst[:, 0] = -128.0

    nc = build_program()
    in_maps = []
    for ci in range(NCORES):
        bidx, j = ci // 4, ci % 4
        t0 = j * TCORE
        if j == 0:
            xs = np.concatenate([np.zeros((HALO, D), np.float32), x[bidx, 0:TCORE]], axis=0)
            hm = np.zeros((128, 2), np.float32)
        else:
            xs = x[bidx, t0 - HALO:t0 + TCORE]
            hm = np.ones((128, 2), np.float32)
        in_maps.append({
            "x_fm": np.ascontiguousarray(xs.T),
            "wq": wq, "wk": wk, "wi": wi, "wo": wo, "fw": fw,
            "fbias": fbias, "gwin": gwin, "ident_in": ident,
            "causal_in": causal, "halom_in": hm, "onesc_in": onesc,
            "mconst_in": mconst,
        })
    res = run_bass_kernel_spmd(nc, in_maps, core_ids=list(range(NCORES)))
    out = np.empty((B, L, D), dtype=np.float32)
    for ci in range(NCORES):
        bidx, j = ci // 4, ci % 4
        out[bidx, j * TCORE:(j + 1) * TCORE, :] = res.results[ci]["y_fm"].T
    if _DBG:
        kernel.dbg = [{k: res.results[ci]["dbg_" + k] for k in _DBG}
                      for ci in range(NCORES)]
    return out
